# version 01
# baseline (speedup 1.0000x reference)
import numpy as np
import concourse.bass as bass
import concourse.mybir as mybir

F32 = mybir.dt.float32
BF16 = mybir.dt.bfloat16
AF = mybir.ActivationFunctionType
ALU = mybir.AluOpType
AX = mybir.AxisListType.X

ENGS = ("sync", "tensor", "vector", "scalar", "gpsimd")


class Ticket:
    __slots__ = ("sem", "val")
    def __init__(self, sem, val):
        self.sem = sem
        self.val = val


class SemCtr:
    def __init__(self, stk, nc, name):
        self.h = stk.enter_context(nc.semaphore(name))
        self.n = 0
    def ticket(self):
        return Ticket(self.h, self.n)


_FENCED_OPS = {
    "tensor_copy", "tensor_tensor", "tensor_scalar", "scalar_tensor_tensor",
    "tensor_reduce", "reduce_max", "reduce_sum", "reciprocal", "memset",
    "matmul", "transpose", "activation", "copy", "mul", "add", "sqrt",
    "square", "sign", "select", "iota", "tensor_add", "tensor_mul",
    "tensor_sub", "tensor_max", "tensor_scalar_add", "tensor_scalar_mul",
    "tensor_scalar_max", "tensor_scalar_min", "tensor_scalar_sub",
    "bn_stats", "bn_aggr", "tensor_tensor_scan", "memzero",
}


class _ParanoidEngine:

    def __init__(self, eng, sem):
        self._e = eng
        self._sem = sem
        self._n = 0
        self._last = None
        self._extra = None

    def swap_fence(self, instr, sem_h, inc_by, sched_val):
        assert instr is self._last, "emitter inc must go on the last emitted op"
        instr.ins.sync_info.on_update = []
        self._n -= 1
        instr.then_inc(sem_h, inc_by)
        self._extra = (sem_h, sched_val)

    def __getattr__(self, name):
        real = getattr(self._e, name)
        if name not in _FENCED_OPS or not callable(real):
            return real
        def wrapped(*a, **kw):
            if self._extra is not None:
                self._e.wait_ge(*self._extra)
                self._extra = None
            if self._n > 0:
                self._e.wait_ge(self._sem, self._n)
            instr = real(*a, **kw)
            if instr is not None:
                instr.then_inc(self._sem, 1)
                self._n += 1
                self._last = instr
            return instr
        return wrapped


class Prog:

    def __init__(self, name="mk", n_cores=8, paranoid=False):
        self.paranoid = paranoid
        self.nc = bass.Bass("TRN2", target_bir_lowering=False,
                            detect_race_conditions=paranoid, num_devices=n_cores)
        from contextlib import ExitStack
        self.stk = ExitStack()
        self.ops = {e: [] for e in ENGS}
        self._sems = {}

    def sem(self, name):
        if name not in self._sems:
            self._sems[name] = SemCtr(self.stk, self.nc, name)
        return self._sems[name]

    def sbuf(self, name, shape, dtype):
        return self.stk.enter_context(self.nc.sbuf_tensor(name, shape, dtype))

    def psum(self, name, shape, dtype=F32):
        return self.stk.enter_context(self.nc.psum_tensor(name, shape, dtype))

    def dram(self, name, shape, dtype, kind=None):
        if kind is None:
            t = self.nc.dram_tensor(name, shape, dtype)
        else:
            t = self.nc.dram_tensor(name, shape, dtype, kind=kind)
        return t.ap()

    def emit(self, eng, fn, waits=(), inc=None, inc_by=1):
        waits = tuple(w for w in waits if w is not None)
        if inc is not None:
            inc.n += inc_by
            sched_val = inc.n
        else:
            sched_val = None
        def closure(e, waits=waits, fn=fn, inc=inc, inc_by=inc_by, sched_val=sched_val):
            for w in waits:
                e.wait_ge(w.sem, w.val)
            instr = fn(e)
            if inc is not None:
                assert instr is not None, "inc requires an instruction"
                if isinstance(e, _ParanoidEngine) and inc_by < 16:
                    e.swap_fence(instr, inc.h, inc_by, sched_val)
                else:
                    instr.then_inc(inc.h, inc_by)
        self.ops[eng].append(closure)
        return None if inc is None else inc.ticket()

    def barrier_wait(self, eng, *tickets):
        ts = tuple(t for t in tickets if t is not None)
        def closure(e, ts=ts):
            for t in ts:
                e.wait_ge(t.sem, t.val)
        self.ops[eng].append(closure)

    def finalize(self):
        nc = self.nc
        if self.paranoid:
            psems = {e: self.sem("paranoid_" + e).h for e in ENGS}
        with self.stk:
            with nc.Block() as block:
                for eng in ENGS:
                    lst = self.ops[eng]
                    if not lst:
                        continue
                    def body(e, lst=lst, eng=eng):
                        if self.paranoid:
                            e = _ParanoidEngine(e, psems[eng])
                        for fn in lst:
                            fn(e)
                    getattr(block, eng if eng != "sync" else "sync")(body)
        return nc


"""L0_attn launch: (b,h)-pair sharding, 4 pairs/core.

Per-core inputs:
  xa        [4, 512, 1024] f32  pair-packed token rows ((seg,r) order)
  Wq,Wk,Wv,Wo [1024, 1024] f32 (shared)
  betas_loc [64, 4] f32         betas[dv, pair]
  iden      [128, 128] bf16     identity (host constant)
Output:
  aout      [4, 512, 1024] f32  attention output rows (pre-residual)

Math per pair (b,h), seg s (16 segs = 4 groups x 4):
  P_{q,k,v} = LN(x_rows) @ W_{q,k,v}   (x_rows = the pair's 32 rows/seg)
  q_h = P_q[seg rows].reshape(512, 64) etc (scrambled l=(r,c) view)
  sq = elu(q)+1, sk = elu(k)+1
  mem += sq^T v ; zc += colsum(sk)
  probs = softmax(q k^T / 8) ; attdot = probs @ v
  num = sq @ mem ; den = rowsum(sq) * zc
  att = gate*(num/den) + (1-gate)*attdot ;  out = A @ Wo  (A = scrambled-back)
"""
import numpy as np

SHIFT_A = 10.0
N_PAIRS = 4
N_GRPS = 4
N_SEGS = 4
SKIP_WO = 0
SKIP_SHUF = 0
A_STAGE = 99
N_PROJ = 24


def build_l0_attn(P):
    nc = P.nc
    xa   = P.dram("xa",   [4, 512, 1024], F32, kind="ExternalInput")
    Wq_d = P.dram("Wq",   [1024, 1024], F32, kind="ExternalInput")
    Wk_d = P.dram("Wk",   [1024, 1024], F32, kind="ExternalInput")
    Wv_d = P.dram("Wv",   [1024, 1024], F32, kind="ExternalInput")
    Wo_d = P.dram("Wo",   [1024, 1024], F32, kind="ExternalInput")
    bet  = P.dram("betas_loc", [64, 4], F32, kind="ExternalInput")
    iden_d = P.dram("iden", [128, 128], BF16, kind="ExternalInput")
    aout = P.dram("aout", [4, 512, 1024], F32, kind="ExternalOutput")
    P.dbg = None

    emit_l0_attn(P, dict(xa=xa, Wq=Wq_d, Wk=Wk_d, Wv=Wv_d, Wo=Wo_d,
                         betas_loc=bet, iden=iden_d), aout)
    osem = P.sem("a_osem")
    P.barrier_wait("scalar", osem.ticket())
    return P


def emit_l0_attn(P, ins, aout, tag="a"):
    nc = P.nc
    S = lambda n: tag + "_" + n

    Wq = P.sbuf(S("Wq"), [128, 8 * 1024], BF16)
    Wk = P.sbuf(S("Wk"), [128, 8 * 1024], BF16)
    Wv = P.sbuf(S("Wv"), [128, 8 * 1024], BF16)
    Wo = P.sbuf(S("Wo"), [128, 8 * 1024], BF16)
    bets = P.sbuf(S("bets"), [64, 4], F32)
    gate = P.sbuf(S("gate"), [64, 2], F32)
    xraw = P.sbuf(S("xraw"), [128, 4 * 1024], F32)
    dump = P.sbuf(S("dump"), [128, 1024], BF16)
    mv   = P.sbuf(S("mv"),   [128, 12], F32)
    xln  = P.sbuf(S("xln"),  [128, 4 * 1024], BF16)
    xlnT = P.sbuf(S("xlnT"), [128, 8 * 512], BF16)
    PqT  = P.sbuf(S("PqT"), [128, 8 * 512], BF16)
    PkT  = P.sbuf(S("PkT"), [128, 8 * 512], BF16)
    PvT  = P.sbuf(S("PvT"), [128, 8 * 512], BF16)
    qT   = P.sbuf(S("qT"), [64, 2048], BF16)
    kT   = P.sbuf(S("kT"), [64, 2048], BF16)
    vT   = P.sbuf(S("vT"), [64, 2048], BF16)
    sqT  = P.sbuf(S("sqT"), [64, 2048], BF16)
    skT  = P.sbuf(S("skT"), [64, 2048], F32)
    scrq = P.sbuf(S("scrq"), [64, 2048], F32)
    scrk = P.sbuf(S("scrk"), [64, 2048], F32)
    vtok = P.sbuf(S("vtok"), [128, 4 * 64], BF16)
    sqtok = P.sbuf(S("sqtok"), [128, 4 * 64], BF16)
    probs = P.sbuf(S("probs"), [128, 4 * 512], BF16)
    memS = P.sbuf(S("memS"), [64, 64], BF16)
    memF = P.sbuf(S("memF"), [64, 64], F32)
    zc   = P.sbuf(S("zc"), [64, 4], F32)
    rsq  = P.sbuf(S("rsq"), [1, 512], F32)
    rden = P.sbuf(S("rden"), [1, 512], F32)
    scrC = P.sbuf(S("scrC"), [64, 2 * 512], F32)
    att4T = P.sbuf(S("att4T"), [64, 2048], BF16)
    AT   = P.sbuf(S("AT"), [128, 8 * 128], BF16)
    outS = P.sbuf(S("outS"), [128, 1024], F32)
    onesc = P.sbuf(S("onesc"), [128, 1], BF16)
    onesr = P.sbuf(S("onesr"), [1, 64], F32)
    shiftc = P.sbuf(S("shiftc"), [128, 1], F32)

    pA  = P.psum(S("pA"), [128, 512], F32)
    pB  = P.psum(S("pB"), [128, 512], F32)
    pS0 = P.psum(S("pS0"), [128, 512], F32)
    pNA = P.psum(S("pNA"), [128, 512], F32)
    pSm = P.psum(S("pSm"), [128, 512], F32)
    pRT = P.psum(S("pRT"), [128, 512], F32)
    pMf = P.psum(S("pMf"), [128, 512], F32)
    pM  = pMf[0:64, 0:64]

    ld   = P.sem(S("ld"))
    ldw  = P.sem(S("ldw"))
    tsem = P.sem(S("tsem"))
    vsem = P.sem(S("vsem"))
    ssem = P.sem(S("ssem"))
    osem = P.sem(S("osem"))
    tx   = P.sem(S("tx"))
    vf   = P.sem(S("vf"))
    vfc  = {"n": 0}
    def VF(v, instr):
        if P.paranoid:
            return
        vfc["n"] += 1
        instr.then_inc(vf.h, 1)
        v.wait_ge(vf.h, vfc["n"])

    E = P.emit

    for Wdst, key in ((Wq, "Wq"), (Wk, "Wk"), (Wv, "Wv"), (Wo, "Wo")):
        for ch in range(8):
            E("gpsimd", lambda g, Wdst=Wdst, key=key, ch=ch: g.dma_start(
                out=Wdst[:, 1024 * ch:1024 * (ch + 1)],
                in_=ins[key][128 * ch:128 * (ch + 1), :]),
              inc=ldw, inc_by=16)
    tW = ldw.ticket()
    misc = P.sem(S("misc"))
    E("sync", lambda sy: sy.dma_start(out=bets[:], in_=ins["betas_loc"][:, :]), inc=misc, inc_by=16)
    tMisc = misc.ticket()
    E("vector", lambda v: v.memset(onesc[:], 1.0))
    E("vector", lambda v: v.memset(onesr[:], 1.0))
    E("vector", lambda v: v.memset(shiftc[:], -SHIFT_A), inc=vsem)
    tOnes = vsem.ticket()

    last = {
        "xraw_free": None, "xln_free": None, "xlnT_free": None,
        "PT_free": None, "qkv_free": None, "att4T_free": None,
        "AT_free": None, "outS_free": None, "pAB_free": None,
        "probs_free": None, "vtok_free": None, "pNA_free": None,
        "pRT_free": None, "pSm_free": None, "pS0_free": None,
        "pTX_free": None, "memS_free": None, "sqT_free": None,
    }

    for pair in range(N_PAIRS):
        _emit_pair(P, locals(), pair)

    return osem


def _emit_pair(P, L, pair):
    E = P.emit
    g = L.__getitem__
    ins, aout = g("ins"), g("aout")
    (Wq, Wk, Wv, Wo, bets, gate, xraw, dump, mv, xln, xlnT, PqT, PkT, PvT,
     qT, kT, vT, sqT, skT, scrq, scrk, vtok, sqtok, probs, memS, zc, rsq, rden,
     scrC, att4T, AT, outS, onesc, onesr, shiftc, memF) = (
        g("Wq"), g("Wk"), g("Wv"), g("Wo"), g("bets"), g("gate"),
        g("xraw"), g("dump"), g("mv"), g("xln"), g("xlnT"), g("PqT"), g("PkT"),
        g("PvT"), g("qT"), g("kT"), g("vT"), g("sqT"), g("skT"), g("scrq"),
        g("scrk"), g("vtok"), g("sqtok"), g("probs"), g("memS"), g("zc"),
        g("rsq"), g("rden"), g("scrC"), g("att4T"), g("AT"), g("outS"),
        g("onesc"), g("onesr"), g("shiftc"), g("memF"))
    pA, pB, pS0, pNA, pSm, pRT, pM = (g("pA"), g("pB"), g("pS0"), g("pNA"),
                                      g("pSm"), g("pRT"), g("pM"))
    ld, ldw, tsem, vsem, ssem, osem, tx = (g("ld"), g("ldw"), g("tsem"), g("vsem"),
                                           g("ssem"), g("osem"), g("tx"))
    tW, tMisc, tOnes = g("tW"), g("tMisc"), g("tOnes")
    VF = g("VF")
    last = g("last")

    for grp in range(4):
        E("sync", lambda sy, grp=grp: sy.dma_start(
            out=xraw[:, 1024 * grp:1024 * (grp + 1)],
            in_=ins["xa"][pair, 128 * grp:128 * (grp + 1), :]),
          waits=(last["xraw_free"],) if grp == 0 else (), inc=ld, inc_by=16)
    tL = ld.ticket()
    if A_STAGE <= 1:
        return

    E("scalar", lambda s: s.activation(gate[:, 0:1], bets[:, pair:pair + 1], AF.Sigmoid),
      waits=(tMisc,), inc=ssem)
    tGate = E("vector", lambda v: v.tensor_scalar(gate[:, 1:2], gate[:, 0:1], -1.0, 1.0,
                                                  op0=ALU.mult, op1=ALU.add),
              waits=(ssem.ticket(),), inc=vsem)
    if A_STAGE <= 2:
        return

    def ln_v1(v):
        for grp in range(4):
            xg = xraw[:, 1024 * grp:1024 * (grp + 1)]
            m = mv[:, grp:grp + 1]
            VF(v, v.tensor_reduce(m, xg, axis=AX, op=ALU.add))
            VF(v, v.tensor_scalar(m, m, 1.0 / 1024.0, 0.0, op0=ALU.mult, op1=ALU.add))
            v.tensor_scalar(xg, xg, m, 0.0, op0=ALU.subtract, op1=ALU.add)
            sq = xln[:, 1024 * grp:1024 * (grp + 1)]
            v.tensor_tensor(sq, xg, xg, op=ALU.mult)
            VF(v, v.tensor_reduce(mv[:, 4 + grp:5 + grp], sq, axis=AX, op=ALU.add))
            _i = v.tensor_scalar(mv[:, 4 + grp:5 + grp], mv[:, 4 + grp:5 + grp],
                                 1.0 / 1024.0, 1e-5, op0=ALU.mult, op1=ALU.add)
        return _i
    tV1 = E("vector", ln_v1, waits=(tL,), inc=vsem)

    def ln_s2(s):
        for grp in range(4):
            _i = s.activation(mv[:, 8 + grp:9 + grp], mv[:, 4 + grp:5 + grp], AF.Sqrt)
        return _i
    tS2 = E("scalar", ln_s2, waits=(tV1,), inc=ssem)

    def ln_v3(v):
        for grp in range(4):
            VF(v, v.reciprocal(mv[:, 8 + grp:9 + grp], mv[:, 8 + grp:9 + grp]))
        for grp in range(4):
            _i = v.tensor_scalar(xln[:, 1024 * grp:1024 * (grp + 1)],
                                 xraw[:, 1024 * grp:1024 * (grp + 1)],
                                 mv[:, 8 + grp:9 + grp], 0.0, op0=ALU.mult, op1=ALU.add)
        return _i
    tV3 = E("vector", ln_v3, waits=(tS2, last["xln_free"]), inc=vsem)
    last["xraw_free"] = tV3
    if A_STAGE <= 3:
        return

    for grp in range(4):
        for ch in range(8):
            E("sync", lambda sy, grp=grp, ch=ch: sy.dma_start_transpose(
                xlnT[:, 512 * ch + 128 * grp:512 * ch + 128 * grp + 128],
                xln[:, 1024 * grp + 128 * ch:1024 * grp + 128 * ch + 128]),
              waits=(tV3, last["xlnT_free"]) if (grp == 0 and ch == 0) else (),
              inc=tx, inc_by=16)
    tXT = tx.ticket()
    if A_STAGE <= 4:
        return

    evac_t = {}
    bank = [pA, pB]
    bank_free = [last["pAB_free"], last["pAB_free"]]
    proj_out = {"q": PqT, "k": PkT, "v": PvT}
    Wsb = {"q": Wq, "k": Wk, "v": Wv}
    idx = 0
    for wname in ("q", "k", "v"):
        if idx >= N_PROJ: break
        for ft in range(8):
            if idx >= N_PROJ: break
            pb = bank[idx % 2]
            def tfn(te, wname=wname, ft=ft, pb=pb):
                for ch in range(8):
                    _i = te.matmul(pb[:, :],
                                   Wsb[wname][:, 1024 * ch + 128 * ft:1024 * ch + 128 * ft + 128],
                                   xlnT[:, 512 * ch:512 * (ch + 1)],
                                   start=(ch == 0), stop=(ch == 7))
                return _i
            w = [tXT, tW] if idx < 2 else [bank_free[idx % 2]]
            tP = E("tensor", tfn, waits=tuple(t for t in w if t), inc=tsem)
            def vfn(v, wname=wname, ft=ft, pb=pb):
                return v.tensor_copy(proj_out[wname][:, 512 * ft:512 * (ft + 1)], pb[:, :])
            tE_ = E("vector", vfn, waits=(tP, last["PT_free"] if idx < 2 else None), inc=vsem)
            bank_free[idx % 2] = tE_
            evac_t[(wname, ft)] = tE_
            idx += 1
    tProj = evac_t[max(evac_t.keys(), key=lambda k: list(evac_t).index(k))] if N_PROJ < 24 else evac_t[("v", 7)]
    last["pAB_free"] = tProj
    last["xln_free"] = tXT
    last["xlnT_free"] = tP

    def tap(name, src_ap, after, eng="scalar"):
        if P.dbg is not None and pair == 0:
            E(eng, lambda s, n=name, sp=src_ap: s.dma_start(out=P.dbg[n][:, :], in_=sp),
              waits=(after,), inc=osem, inc_by=16)
    if pair == 0 and P.dbg is not None:
        tap("d_xln", xln[:, 0:4096], tV3)
        tap("d_mv", mv[:, :], tV3)
        tap("d_xraw", xraw[:, 0:4096], tV3)
        tap("d_xlnT", xlnT[:, 0:4096], tXT)
        tap("d_PqT", PqT[:, 0:4096], tProj)

    tNA_last = None
    tD_last = None
    tC_last = None
    tA_prev = last["AT_free"]
    first_mem = (pair == 0)
    for grp in range(N_GRPS):
        def shfn(v, grp=grp):
            for tens_src, tens_dst in ((PqT, qT), (PkT, kT), (PvT, vT)):
                for c in range(16):
                    src = tens_src[64 * (c % 2):64 * (c % 2) + 64,
                                   512 * (c // 2) + 128 * grp:512 * (c // 2) + 128 * grp + 128]
                    dst = tens_dst[0:64, c:2048:16]
                    _i = v.tensor_copy(dst, src)
            return _i
        wsh = [tProj, last["qkv_free"]]
        if SKIP_SHUF:
            tSh = E("vector", lambda v: v.nop(), waits=tuple(t for t in wsh if t), inc=vsem)
        else:
            tSh = E("vector", shfn, waits=tuple(t for t in wsh if t), inc=vsem)

        def elu_v1(v):
            v.tensor_scalar(scrq[:, :], qT[:, :], 0.0, 0.0, op0=ALU.min, op1=ALU.add)
            return v.tensor_scalar(scrk[:, :], kT[:, :], 0.0, 0.0, op0=ALU.min, op1=ALU.add)
        tVe1 = E("vector", elu_v1, waits=(tSh,), inc=vsem)
        def elu_s(s):
            s.activation(scrq[:, :], scrq[:, :], AF.Exp)
            return s.activation(scrk[:, :], scrk[:, :], AF.Exp)
        tSe = E("scalar", elu_s, waits=(tVe1,), inc=ssem)
        def elu_v2(v):
            v.scalar_tensor_tensor(sqT[:, :], qT[:, :], 0.0, scrq[:, :],
                                   op0=ALU.max, op1=ALU.add)
            return v.scalar_tensor_tensor(skT[:, :], kT[:, :], 0.0, scrk[:, :],
                                          op0=ALU.max, op1=ALU.add)
        tVe2 = E("vector", elu_v2, waits=(tSe, last.get("sqT_free")), inc=vsem)
        if grp == 0:
            tap("d_qT", qT[:, :], tSh)
            tap("d_kT", kT[:, :], tSh)
            tap("d_sqT", sqT[:, :], tVe2)

        for s_ in range(N_SEGS):
            seg_first = (grp == 0 and s_ == 0)
            seg_last = (grp == 3 and s_ == 3)
            sl = slice(512 * s_, 512 * (s_ + 1))
            def zcfn(v, s_=s_, sl=sl, seg_first=seg_first):
                VF(v, v.tensor_reduce(zc[:, 1:2], skT[:, sl], axis=AX, op=ALU.add))
                if seg_first:
                    VF(v, v.tensor_copy(zc[:, 0:1], zc[:, 1:2]))
                else:
                    VF(v, v.tensor_tensor(zc[:, 0:1], zc[:, 0:1], zc[:, 1:2], op=ALU.add))
                VF(v, v.reciprocal(zc[:, 2:3], zc[:, 0:1]))
                return v.tensor_tensor(zc[:, 3:4], zc[:, 2:3], gate[:, 0:1], op=ALU.mult)
            tZ = E("vector", zcfn, waits=(tVe2,), inc=vsem)

            for i in range(4):
                E("sync", lambda sy, i=i, sl=sl: sy.dma_start_transpose(
                    vtok[:, 64 * i:64 * (i + 1)], vT[:, sl][:, 128 * i:128 * (i + 1)]),
                  waits=(tVe2, last["vtok_free"]) if i == 0 else (), inc=tx, inc_by=16)
            for i in range(4):
                E("sync", lambda sy, i=i, sl=sl: sy.dma_start_transpose(
                    sqtok[:, 64 * i:64 * (i + 1)], sqT[:, sl][:, 128 * i:128 * (i + 1)]),
                  inc=tx, inc_by=16)
            tEv = tx.ticket()

            if grp == 0 and s_ == 0:
                tap("d_vtok", vtok[:, :], tEv)
            def memfn(te):
                for i in range(4):
                    _i = te.matmul(pM, sqtok[:, 64 * i:64 * (i + 1)],
                                   vtok[:, 64 * i:64 * (i + 1)],
                                   start=(i == 0), stop=(i == 3))
                return _i
            tM = E("tensor", memfn, waits=(tEv, last["memS_free"]), inc=tsem)
            def memacc(v, seg_first=seg_first):
                if seg_first:
                    VF(v, v.tensor_copy(memF[:, :], pM))
                else:
                    VF(v, v.tensor_tensor(memF[:, :], memF[:, :], pM, op=ALU.add))
                return v.tensor_copy(memS[:, :], memF[:, :])
            tMe = E("vector", memacc, waits=(tM,), inc=vsem)

            tEx = [None] * 4
            for lt in range(4):
                def scfn(te, lt=lt, sl=sl):
                    return te.matmul(pS0[:, :], kT[:, sl][:, 128 * lt:128 * (lt + 1)],
                                     qT[:, sl], start=True, stop=True)
                wsc = [tSh if lt == 0 else None, last["pS0_free"]]
                tSc = E("tensor", scfn, waits=tuple(t for t in wsc if t), inc=tsem)
                def exfn(s, lt=lt):
                    return s.activation(probs[:, 512 * lt:512 * (lt + 1)], pS0[:, :],
                                        AF.Exp, scale=0.125, bias=shiftc[:, 0:1])
                wex = [tSc, tOnes if lt == 0 else None, last["probs_free"] if lt == 0 else None]
                tEx[lt] = E("scalar", exfn, waits=tuple(t for t in wex if t), inc=ssem)
                last["pS0_free"] = tEx[lt]

            if grp == 0 and s_ == 0:
                tap("d_probs", probs[:, :], tEx[3])
            def denfn(te):
                for lt in range(4):
                    te.matmul(pSm[0:1, :], onesc[:, :], probs[:, 512 * lt:512 * (lt + 1)],
                              start=(lt == 0), stop=(lt == 3))
                return te.matmul(pSm[32:33, :], onesc[0:64, :], sqT[:, sl], start=True, stop=True)
            wden = [tEx[3], tOnes, last["pSm_free"]]
            tD = E("tensor", denfn, waits=tuple(t for t in wden if t), inc=tsem)

            def recfn(v):
                VF(v, v.reciprocal(rden[:, :], pSm[0:1, :]))
                return v.reciprocal(rsq[:, :], pSm[32:33, :])
            tR = E("vector", recfn, waits=(tD,), inc=vsem)
            last["pSm_free"] = tR

            def repfn(te):
                te.matmul(pRT[0:64, :], onesr[:, :], rsq[:, :], start=True, stop=True)
                return te.matmul(pRT[64:128, :], onesr[:, :], rden[:, :], start=True, stop=True)
            wrep = [tR, last["pRT_free"]]
            tRp = E("tensor", repfn, waits=tuple(t for t in wrep if t), inc=tsem)
            def numfn(te, sl=sl):
                te.matmul(pNA[0:64, :], memS[:, :], sqT[:, sl], start=True, stop=True)
                for lt in range(4):
                    _i = te.matmul(pNA[64:128, :], vtok[:, 64 * lt:64 * (lt + 1)],
                                   probs[:, 512 * lt:512 * (lt + 1)],
                                   start=(lt == 0), stop=(lt == 3))
                return _i
            wnum = [tMe, tEx[3], last["pNA_free"]]
            tNA = E("tensor", numfn, waits=tuple(t for t in wnum if t), inc=tsem)
            tNA_last = tNA
            tD_last = tD
            last["sqT_free"] = tNA
            last["vtok_free"] = tNA
            last["memS_free"] = tNA
            last["probs_free"] = tNA

            def cmbfn(v, s_=s_, sl=sl):
                v.tensor_scalar(scrC[:, 0:512], pNA[0:64, :], zc[:, 3:4], 0.0,
                                op0=ALU.mult, op1=ALU.add)
                v.tensor_tensor(scrC[:, 0:512], scrC[:, 0:512], pRT[0:64, :], op=ALU.mult)
                v.tensor_scalar(scrC[:, 512:1024], pNA[64:128, :], gate[:, 1:2], 0.0,
                                op0=ALU.mult, op1=ALU.add)
                v.tensor_tensor(scrC[:, 512:1024], scrC[:, 512:1024], pRT[64:128, :], op=ALU.mult)
                return v.tensor_tensor(att4T[:, sl], scrC[:, 0:512], scrC[:, 512:1024], op=ALU.add)
            wcmb = [tNA, tRp, tZ, last["att4T_free"] if (s_ == 0 and grp == 0) else None]
            tC = E("vector", cmbfn, waits=tuple(t for t in wcmb if t), inc=vsem)
            tC_last = tC
            last["pNA_free"] = tC
            last["pRT_free"] = tC

        if grp == 0:
            tap("d_att4T", att4T[:, :], tC_last)
            tap("d_mem", memF[:, :], tC_last)
        def atfn(v):
            for c in range(16):
                ch2, j = c // 2, c % 2
                src = att4T[0:64, c:2048:16]
                dst = AT[64 * j:64 * j + 64, 128 * ch2:128 * (ch2 + 1)]
                _i = v.tensor_copy(dst, src)
            return _i
        wat = [tC_last, tA_prev]
        tA = E("vector", atfn, waits=tuple(t for t in wat if t), inc=vsem)
        last["qkv_free"] = tD_last
        att4T_free_next = tA

        if grp == 0:
            tap("d_AT", AT[:, :], tA)
        tWo = [None, None]
        if SKIP_WO:
            for half in range(2):
                tWo[half] = E("tensor", lambda te: te.nop(), waits=(tA,), inc=tsem)
        for half in range(2) if not SKIP_WO else []:
            pb = pA if half == 0 else pB
            def wofn(te, half=half, pb=pb):
                for ch2 in range(8):
                    _i = te.matmul(pb[:, :], AT[:, 128 * ch2:128 * (ch2 + 1)],
                                   Wo[:, 1024 * ch2 + 512 * half:1024 * ch2 + 512 * half + 512],
                                   start=(ch2 == 0), stop=(ch2 == 7))
                return _i
            wwo = [tA, last["pAB_free"]]
            tWo[half] = E("tensor", wofn, waits=tuple(t for t in wwo if t), inc=tsem)
        def oefn(v):
            v.tensor_copy(outS[:, 0:512], pA[:, :])
            return v.tensor_copy(outS[:, 512:1024], pB[:, :])
        woe = [tWo[1], last["outS_free"]]
        tOe = E("vector", oefn, waits=tuple(t for t in woe if t), inc=vsem)
        last["pAB_free"] = tOe
        last["AT_free"] = tWo[1]
        tA_prev = tWo[1]
        tOut = E("scalar", lambda s, grp=grp: s.dma_start(
            out=aout[pair, 128 * grp:128 * (grp + 1), :], in_=outS[:, :]),
            waits=(tOe,), inc=osem, inc_by=16)
        last["outS_free"] = tOut
        last["att4T_free"] = att4T_free_next

    last["PT_free"] = tD_last


"""L0_ffn launch: token-contiguous sharding, 2048 own + 128 halo tokens.

Per-core inputs (host-sliced / layout only):
  xb       [2176, 1024] f32 : raw x rows (halo first 128, then own 2048; core 0/4 halo = zeros)
  ab       [2176, 1024] f32 : L0_attn outputs for the same rows
  qkvw     [1024, 1536] f32 : qkv_w with COLUMNS PERMUTED (q-e 512 | q-o 512 | k-e 128 | k-o 128 | v 256)
  qkvb     [1, 1536]   f32  : qkv_b same permutation
  projw    [1024, 1024] f32, projb [1, 1024] f32
  anw      [128, 8] f32     : attn_norm_w chunk-major ([d%128, d//128])
  fnw      [128, 8] f32     : ffn_norm_w chunk-major
  gatew    [1024, 32] f32, gateb [1, 32] f32
  Wup      [32, 1024, 2048] f32, Wdn [32, 1024, 1024] f32
  sink     [1, 16] f32
  costab   [2176, 512] f32  : cos(pos/32 * invfreq[j]) at (token, (h,j))
  sintab   [2176, 512] f32
  maskd    [128, 128] f32   : causal additive mask (0 / -3e38-ish -> use -30000)
  maskl    [128, 128] f32   : left-block mask for tiles >= 2 (j' >= q')
  maskl0   [128, 128] f32   : left mask for tile 1 (per-core: cores 0/4 all-masked)
Output:
  bout     [2048, 1024] f32 : x2 rows (own tokens)
Scratch DRAM: x1d, xid, xi2d  [2176, 1024] f32 per-core internal.
"""
import numpy as np

NT = 17
SHIFT_B = 12.0
NEG = -30000.0


def build_l0_ffn(P):
    ins = {}
    for name, shape in [
        ("xb", [2176, 1024]), ("ab", [2176, 1024]),
        ("qkvw", [1024, 1536]), ("qkvb", [1, 1536]),
        ("projw", [1024, 1024]), ("projb", [1, 1024]),
        ("anw", [128, 8]), ("fnw", [128, 8]),
        ("gatew", [1024, 32]), ("gateb", [1, 32]),
        ("Wup", [32, 1024, 2048]), ("Wdn", [32, 1024, 1024]),
        ("sink", [1, 16]),
        ("costab", [2176, 1024]), ("sintab", [2176, 1024]),
        ("maskd", [128, 128]), ("maskl", [128, 128]), ("maskl0", [128, 128]),
    ]:
        ins[name] = P.dram(name, shape, F32, kind="ExternalInput")
    bout = P.dram("bout", [2048, 1024], F32, kind="ExternalOutput")
    P.dbg = None
    emit_l0_ffn(P, ins, bout)
    osem = P.sem("b_osem")
    P.barrier_wait("scalar", osem.ticket())
    return P


def emit_l0_ffn(P, ins, bout, tag="b"):
    nc = P.nc
    S = lambda n: tag + "_" + n
    E = P.emit

    x1d = P.dram(S("x1d"), [2176, 1024], F32)
    xi2d = P.dram(S("xi2d"), [2176, 1024], F32)

    xt   = P.sbuf(S("xt"),  [128, 1024], F32)
    at   = P.sbuf(S("at"),  [128, 1024], F32)
    x1t  = P.sbuf(S("x1t"), [128, 1024], F32)
    mv   = P.sbuf(S("mv"),  [128, 8], F32)
    ht   = P.sbuf(S("ht"),  [128, 1024], BF16)
    hTt  = P.sbuf(S("hTt"), [128, 8 * 128], BF16)
    h2T  = P.sbuf(S("h2T"), [128, 8 * 2048], BF16)
    anws = P.sbuf(S("anws"), [128, 8], F32)
    fnws = P.sbuf(S("fnws"), [128, 8], F32)
    qkvw = P.sbuf(S("qkvw"), [128, 8 * 1536], BF16)
    qkvb = P.sbuf(S("qkvb"), [1, 1536], BF16)
    projw = P.sbuf(S("projw"), [128, 8 * 1024], BF16)
    projb = P.sbuf(S("projb"), [1, 1024], BF16)
    gatew = P.sbuf(S("gatew"), [128, 8 * 32], BF16)
    gateb = P.sbuf(S("gateb"), [1, 32], BF16)
    onesr = P.sbuf(S("onesr"), [1, 128], BF16)
    onesc = P.sbuf(S("onesc"), [128, 1], BF16)
    onesrf = P.sbuf(S("onesrf"), [1, 64], F32)
    shiftc = P.sbuf(S("shiftc"), [128, 1], F32)
    sinkS = P.sbuf(S("sinkS"), [1, 16], F32)
    sinkE = P.sbuf(S("sinkE"), [1, 16], F32)
    cost  = P.sbuf(S("cost"), [128, 1024], BF16)
    sint  = P.sbuf(S("sint"), [128, 1024], BF16)
    maskd = P.sbuf(S("maskd"), [128, 128], BF16)
    maskl = P.sbuf(S("maskl"), [128, 128], BF16)
    maskl0 = P.sbuf(S("maskl0"), [128, 128], BF16)
    qkvS = P.sbuf(S("qkvS"), [128, 1280], BF16)
    scrA = P.sbuf(S("scrA"), [128, 256], BF16)
    scrB = P.sbuf(S("scrB"), [128, 256], BF16)
    qTt  = P.sbuf(S("qTt"), [128, 8 * 128], BF16)
    kT2  = P.sbuf(S("kT2"), [128, 2 * 2176], BF16)
    vtok = P.sbuf(S("vtok"), [128, 17 * 256], BF16)
    probs = P.sbuf(S("probs"), [128, 2 * 128], BF16)
    rden = P.sbuf(S("rden"), [1, 128], F32)
    den  = P.sbuf(S("den"), [1, 128], F32)
    yTt  = P.sbuf(S("yTt"), [128, 8 * 128], BF16)
    glog = P.sbuf(S("glog"), [128, 32], F32)
    tv   = P.sbuf(S("tv"), [128, 16], F32)
    gmask = P.sbuf(S("gmask"), [128, 4 * 32], BF16)
    comb = P.sbuf(S("comb"), [128, 16 * 32], BF16)
    wup  = P.sbuf(S("wup"), [128, 2 * 8 * 1024], BF16)
    wdn  = P.sbuf(S("wdn"), [128, 8 * 1024], BF16)
    uT   = P.sbuf(S("uT"), [128, 8 * 512], BF16)
    sT   = P.sbuf(S("sT"), [128, 8 * 512], BF16)
    macc = P.sbuf(S("macc"), [128, 4 * 1024], F32)
    outt = x1t

    pQ  = P.psum(S("pQ"), [128, 512], F32)
    pQ2 = P.psum(S("pQ2"), [128, 512], F32)
    pQ3 = P.psum(S("pQ3"), [128, 512], F32)
    pS  = P.psum(S("pS"), [128, 512], F32)
    pY  = P.psum(S("pY"), [128, 512], F32)
    pP  = P.psum(S("pP"), [128, 512], F32)
    pP2 = P.psum(S("pP2"), [128, 512], F32)
    pSm = P.psum(S("pSm"), [128, 512], F32)

    ld   = P.sem(S("ld"))
    ldx  = P.sem(S("ldx"))
    lda  = P.sem(S("lda"))
    ldc  = P.sem(S("ldc"))
    lds2 = P.sem(S("lds2"))
    ldm  = P.sem(S("ldm"))
    ldu0 = P.sem(S("ldu0"))
    ldu1 = P.sem(S("ldu1"))
    ldd  = P.sem(S("ldd"))
    ldg  = P.sem(S("ldg"))
    tx   = P.sem(S("tx"))
    txq  = P.sem(S("txq"))
    txk  = P.sem(S("txk"))
    txh2 = P.sem(S("txh2"))
    ost  = P.sem(S("ost"))
    tsem = P.sem(S("tsem"))
    vsem = P.sem(S("vsem"))
    ssem = P.sem(S("ssem"))
    osem = P.sem(S("osem"))
    vf   = P.sem(S("vf"))
    vfc = {"n": 0}
    def VF(v, instr):
        if P.paranoid:
            return
        vfc["n"] += 1
        instr.then_inc(vf.h, 1)
        v.wait_ge(vf.h, vfc["n"])

    def gw(g):
        for dst, src, chunks, width in ((qkvw, ins["qkvw"], 8, 1536),
                                        (projw, ins["projw"], 8, 1024),
                                        (gatew, ins["gatew"], 8, 32)):
            for ch in range(chunks):
                g.dma_start(out=dst[:, width * ch:width * (ch + 1)],
                            in_=src[128 * ch:128 * (ch + 1), :]).then_inc(ldg.h, 16)
        for dst, src in ((qkvb, ins["qkvb"]), (projb, ins["projb"]), (gateb, ins["gateb"])):
            g.dma_start(out=dst[:, :], in_=src[:, :]).then_inc(ldg.h, 16)
        return None
    E("gpsimd", gw)
    ldg.n += 16 * (24 + 3)
    tW = ldg.ticket()
    def sl(sy):
        for dst, src in ((anws, ins["anw"]), (fnws, ins["fnw"]), (sinkS, ins["sink"])):
            sy.dma_start(out=dst[:, :], in_=src[:, :]).then_inc(ld.h, 16)
        return None
    E("sync", sl)
    ld.n += 16 * 3
    def slg(g):
        for dst, src in ((maskd, ins["maskd"]), (maskl, ins["maskl"]), (maskl0, ins["maskl0"])):
            g.dma_start(out=dst[:, :], in_=src[:, :]).then_inc(ldm.h, 16)
        return None
    E("gpsimd", slg)
    ldm.n += 16 * 3
    tMg = ldm.ticket()
    tM = ld.ticket()
    E("vector", lambda v: v.memset(onesr[:], 1.0))
    E("vector", lambda v: v.memset(onesc[:], 1.0))
    E("vector", lambda v: v.memset(onesrf[:], 1.0))
    E("vector", lambda v: v.memset(shiftc[:], -SHIFT_B), inc=vsem)
    tOnes = vsem.ticket()
    tSink = E("scalar", lambda s: s.activation(sinkE[:, :], sinkS[:, :], AF.Exp,
                                               bias=shiftc[0:1, 0:1], scale=1.0),
              waits=(tM, tOnes), inc=ssem)

    last = {}

    tHTw = [None] * NT
    prev_store = None
    for t in range(NT):
        tL = E("sync", lambda sy, t=t: sy.dma_start(out=xt[:, :], in_=ins["xb"][128 * t:128 * (t + 1), :]),
               waits=(last.get("xt_free"),), inc=ldx, inc_by=16)
        tL2 = E("sync", lambda sy, t=t: sy.dma_start(out=at[:, :], in_=ins["ab"][128 * t:128 * (t + 1), :]),
                waits=(last.get("at_free"),), inc=lda, inc_by=16)
        tX1 = E("vector", lambda v: v.tensor_tensor(x1t[:, :], xt[:, :], at[:, :], op=ALU.add),
                waits=(tL, tL2, last.get("x1t_free")), inc=vsem)
        last["xt_free"] = tX1
        tSx1 = E("scalar", lambda s, t=t: s.dma_start(out=x1d[128 * t:128 * (t + 1), :], in_=x1t[:, :]),
                 waits=(tX1,), inc=ost, inc_by=16)
        def lnfn(v, t=t):
            m = mv[:, 0:1]
            VF(v, v.tensor_reduce(m, x1t[:, :], axis=AX, op=ALU.add))
            VF(v, v.tensor_scalar(m, m, 1.0 / 1024.0, 0.0, op0=ALU.mult, op1=ALU.add))
            v.tensor_scalar(x1t[:, :], x1t[:, :], m, 0.0, op0=ALU.subtract, op1=ALU.add)
            v.tensor_tensor(at[:, :], x1t[:, :], x1t[:, :], op=ALU.mult)
            VF(v, v.tensor_reduce(mv[:, 1:2], at[:, :], axis=AX, op=ALU.add))
            return v.tensor_scalar(mv[:, 1:2], mv[:, 1:2], 1.0 / 1024.0, 1e-5, op0=ALU.mult, op1=ALU.add)
        tLNv = E("vector", lnfn, waits=(tSx1,), inc=vsem)
        tLNs = E("scalar", lambda s: s.activation(mv[:, 2:3], mv[:, 1:2], AF.Sqrt), waits=(tLNv,), inc=ssem)
        def xifn(v, t=t):
            VF(v, v.reciprocal(mv[:, 2:3], mv[:, 2:3]))
            v.tensor_scalar(x1t[:, :], x1t[:, :], mv[:, 2:3], 0.0, op0=ALU.mult, op1=ALU.add)
            v.tensor_tensor(at[:, :], x1t[:, :], x1t[:, :], op=ALU.mult)
            VF(v, v.tensor_reduce(mv[:, 3:4], at[:, :], axis=AX, op=ALU.add))
            return v.tensor_scalar(mv[:, 3:4], mv[:, 3:4], 1.0 / 1024.0, 1e-5, op0=ALU.mult, op1=ALU.add)
        tXI = E("vector", xifn, waits=(tLNs,), inc=vsem)
        tXIs = E("scalar", lambda s: s.activation(mv[:, 4:5], mv[:, 3:4], AF.Sqrt), waits=(tXI,), inc=ssem)
        def hfn(v, t=t):
            VF(v, v.reciprocal(mv[:, 4:5], mv[:, 4:5]))
            return v.tensor_scalar(ht[:, :], x1t[:, :], mv[:, 4:5], 0.0, op0=ALU.mult, op1=ALU.add)
        tH = E("vector", hfn, waits=(tXIs, last.get("ht_free")), inc=vsem)
        for ch in range(8):
            tT = E("sync", lambda sy, ch=ch: sy.dma_start_transpose(
                hTt[:, 128 * ch:128 * (ch + 1)], ht[:, 128 * ch:128 * (ch + 1)]),
                waits=(tH, last.get("hTt_free")) if ch == 0 else (), inc=tx, inc_by=16)
        last["ht_free"] = tT
        def hwfn(v):
            for ch in range(8):
                _i = v.tensor_scalar(hTt[:, 128 * ch:128 * (ch + 1)], hTt[:, 128 * ch:128 * (ch + 1)],
                                     anws[:, ch:ch + 1], 0.0, op0=ALU.mult, op1=ALU.add)
            return _i
        tHW = E("vector", hwfn, waits=(tT, tM if t == 0 else None, tMg if t == 0 else None), inc=vsem)

        def qkvfn(te, t=t):
            for nb, pb in ((0, pQ), (1, pQ2), (2, pQ3)):
                for ch in range(8):
                    te.matmul(pb[:, :], hTt[:, 128 * ch:128 * (ch + 1)],
                              qkvw[:, 1536 * ch + 512 * nb:1536 * ch + 512 * nb + 512],
                              start=(ch == 0), stop=False)
                _i = te.matmul(pb[:, :], onesr[:, :], qkvb[:, 512 * nb:512 * (nb + 1)],
                               start=False, stop=True)
            return _i
        wq = [tHW, tW if t == 0 else None, last.get("pQ_free")]
        tQ = E("tensor", qkvfn, waits=tuple(w for w in wq if w), inc=tsem)
        last["hTt_free"] = tQ

        tC = E("gpsimd", lambda g, t=t: g.dma_start(out=cost[:, :], in_=ins["costab"][128 * t:128 * (t + 1), :]),
               waits=(last.get("cos_free"),), inc=ldc, inc_by=16)
        tC2 = E("gpsimd", lambda g, t=t: g.dma_start(out=sint[:, :], in_=ins["sintab"][128 * t:128 * (t + 1), :]),
                inc=lds2, inc_by=16)

        def ropefn(v, t=t):
            for pb, base in ((pQ, 0), (pQ2, 512)):
                pe = pb[:].rearrange("p (b c) -> p b c", c=64)[:, :, 0:32]
                po = pb[:].rearrange("p (b c) -> p b c", c=64)[:, :, 32:64]
                ce = cost[:, base:base + 512].rearrange("p (b c) -> p b c", c=64)[:, :, 0:32]
                se = sint[:, base:base + 512].rearrange("p (b c) -> p b c", c=64)[:, :, 0:32]
                oe = qkvS[:, base:base + 512].rearrange("p (b c) -> p b c", c=64)[:, :, 0:32]
                oo = qkvS[:, base:base + 512].rearrange("p (b c) -> p b c", c=64)[:, :, 32:64]
                sA = scrA[:].rearrange("p (b c) -> p b c", c=32)
                sB = scrB[:].rearrange("p (b c) -> p b c", c=32)
                v.tensor_tensor(sA, pe, ce, op=ALU.mult)
                v.tensor_tensor(sB, po, se, op=ALU.mult)
                v.tensor_tensor(oe, sA, sB, op=ALU.subtract)
                v.tensor_tensor(sA, pe, se, op=ALU.mult)
                v.tensor_tensor(sB, po, ce, op=ALU.mult)
                v.tensor_tensor(oo, sA, sB, op=ALU.add)
            pk = pQ3[:, 0:256].rearrange("p (b c) -> p b c", c=64)
            ck = cost[:, 0:256].rearrange("p (b c) -> p b c", c=64)
            sk = sint[:, 0:256].rearrange("p (b c) -> p b c", c=64)
            ok = qkvS[:, 1024:1280].rearrange("p (b c) -> p b c", c=64)
            sA = scrA[:, 0:128].rearrange("p (b c) -> p b c", c=32)
            sB = scrB[:, 0:128].rearrange("p (b c) -> p b c", c=32)
            v.tensor_tensor(sA, pk[:, :, 0:32], ck[:, :, 0:32], op=ALU.mult)
            v.tensor_tensor(sB, pk[:, :, 32:64], sk[:, :, 0:32], op=ALU.mult)
            v.tensor_tensor(ok[:, :, 0:32], sA, sB, op=ALU.subtract)
            v.tensor_tensor(sA, pk[:, :, 0:32], sk[:, :, 0:32], op=ALU.mult)
            v.tensor_tensor(sB, pk[:, :, 32:64], ck[:, :, 0:32], op=ALU.mult)
            v.tensor_tensor(ok[:, :, 32:64], sA, sB, op=ALU.add)
            return v.tensor_copy(vtok[:, 256 * t:256 * (t + 1)], pQ3[:, 256:512])
        tR = E("vector", ropefn, waits=(tQ, tC, tC2, last.get("qkvS_free")), inc=vsem)
        last["pQ_free"] = tR
        last["cos_free"] = tR

        tK1 = E("sync", lambda sy, t=t: sy.dma_start_transpose(
            kT2[:, 128 * t:128 * (t + 1)], qkvS[:, 1024:1152]),
            waits=(tR, last.get("kT2_gate")), inc=txk, inc_by=16)
        tK2 = E("sync", lambda sy, t=t: sy.dma_start_transpose(
            kT2[:, 2176 + 128 * t:2176 + 128 * (t + 1)], qkvS[:, 1152:1280]), inc=txk, inc_by=16)
        for ch in range(8):
            tQT = E("sync", lambda sy, ch=ch: sy.dma_start_transpose(
                qTt[:, 128 * ch:128 * (ch + 1)], qkvS[:, 128 * ch:128 * (ch + 1)]),
                waits=(last.get("qTt_free"),) if ch == 0 else (), inc=txq, inc_by=16)
        last["qkvS_free"] = tQT

        ktiles = [t] if t == 0 else [t - 1, t]
        yev = [None, None]
        for h in range(16):
            kv = h // 4
            cq = (h % 4) * 2 + (h // 8)
            ko = 64 * (kv % 2)
            kh = kv // 2
            tEx = [None, None]
            for ki, kt in enumerate(ktiles):
                def scfn(te, h=h, ko=ko, kh=kh, cq=cq, kt=kt, ki=ki):
                    return te.matmul(pS[:, 128 * ki:128 * (ki + 1)],
                                     kT2[ko:ko + 64, 2176 * kh + 128 * kt:2176 * kh + 128 * (kt + 1)],
                                     qTt[ko:ko + 64, 128 * cq:128 * (cq + 1)],
                                     start=True, stop=True)
                wsc = [tQT, tK2, last.get("pS_free")] if (h == 0 and ki == 0) else (
                      [last.get("pS_free")] if ki == 0 else [])
                tSc = E("tensor", scfn, waits=tuple(w for w in wsc if w), inc=tsem)
                mask = maskd
                if ki == 0 and t > 0:
                    mask = maskl0 if t == 1 else maskl
                def mexfn(v, ki=ki, mask=mask):
                    return v.tensor_tensor(scrA[:, 128 * ki:128 * (ki + 1)],
                                           pS[:, 128 * ki:128 * (ki + 1)], mask[:, :], op=ALU.add)
                tMx = E("vector", mexfn, waits=(tSc,), inc=vsem)
                def exfn(s, ki=ki):
                    return s.activation(probs[:, 128 * ki:128 * (ki + 1)], scrA[:, 128 * ki:128 * (ki + 1)],
                                        AF.Exp, scale=0.125, bias=shiftc[:, 0:1])
                tEx[ki] = E("scalar", exfn, waits=(tMx, tOnes if (t == 0 and h == 0) else None,
                                                   last.get("probs_free")), inc=ssem)
            last["pS_free"] = tEx[len(ktiles) - 1]
            def denfn(te, nk=len(ktiles)):
                for ki in range(nk):
                    _i = te.matmul(pSm[0:1, 0:128], onesc[:, :],
                                   probs[:, 128 * ki:128 * (ki + 1)], start=(ki == 0), stop=(ki == nk - 1))
                return _i
            tD = E("tensor", denfn, waits=(tEx[len(ktiles) - 1], last.get("pSm_free")), inc=tsem)
            def rdfn(v, h=h):
                VF(v, v.tensor_scalar(den[:, :], pSm[0:1, 0:128], sinkE[:, h:h + 1], 0.0,
                                      op0=ALU.add, op1=ALU.add))
                return v.reciprocal(rden[:, :], den[:, :])
            tRd = E("vector", rdfn, waits=(tD, tSink if (t == 0 and h == 0) else None), inc=vsem)
            last["pSm_free"] = tRd
            def yfn(te, h=h, kv=kv, nk=len(ktiles), kts=tuple(ktiles)):
                for ki, kt in enumerate(kts):
                    _i = te.matmul(pY[0:64, 0:128], vtok[:, 256 * kt + 64 * kv:256 * kt + 64 * kv + 64],
                                   probs[:, 128 * ki:128 * (ki + 1)], start=(ki == 0), stop=(ki == nk - 1))
                return _i
            tY = E("tensor", yfn, waits=(tEx[len(ktiles) - 1], last.get("pY_free")), inc=tsem)
            last["probs_free"] = tY
            def repfn(te):
                return te.matmul(pY[64:128, 0:128], onesrf[:, :], rden[:, :], start=True, stop=True)
            tRp = E("tensor", repfn, waits=(tRd,), inc=tsem)
            def yefn(v, h=h):
                VF(v, v.tensor_copy(scrB[0:64, 0:128], pY[64:128, 0:128]))
                return v.tensor_tensor(yTt[64 * (h % 2):64 * (h % 2) + 64, 128 * (h // 2):128 * (h // 2) + 128],
                                       pY[0:64, 0:128], scrB[0:64, 0:128], op=ALU.mult)
            wye = [tY, tRp, last.get("yTt_free") if h == 0 else None]
            yev[0] = E("vector", yefn, waits=tuple(w for w in wye if w), inc=vsem)
            last["pY_free"] = yev[0]
        tYall = yev[0]
        last["qTt_free"] = tD
        last["kT2_gate"] = tD

        for half in range(2):
            pb = pP if half == 0 else pP2
            def projfn(te, half=half, pb=pb):
                for ch in range(8):
                    te.matmul(pb[:, :], yTt[:, 128 * ch:128 * (ch + 1)],
                              projw[:, 1024 * ch + 512 * half:1024 * ch + 512 * half + 512],
                              start=(ch == 0), stop=False)
                return te.matmul(pb[:, :], onesr[:, :], projb[:, 512 * half:512 * (half + 1)],
                                 start=False, stop=True)
            wpj = [tYall] + ([last.get("pP_free")] if half == 0 else [])
            tPj = E("tensor", projfn, waits=tuple(w for w in wpj if w), inc=tsem)
        last["yTt_free"] = tPj
        def xi2fn(v, t=t):
            v.tensor_tensor(x1t[:, 0:512], x1t[:, 0:512], pP[:, :], op=ALU.add)
            v.tensor_tensor(x1t[:, 512:1024], x1t[:, 512:1024], pP2[:, :], op=ALU.add)
            v.tensor_tensor(at[:, :], x1t[:, :], x1t[:, :], op=ALU.mult)
            VF(v, v.tensor_reduce(mv[:, 5:6], at[:, :], axis=AX, op=ALU.add))
            return v.tensor_scalar(mv[:, 5:6], mv[:, 5:6], 1.0 / 1024.0, 1e-5, op0=ALU.mult, op1=ALU.add)
        tXi2 = E("vector", xi2fn, waits=(tPj,), inc=vsem)
        last["at_free"] = tXi2
        last["pP_free"] = tXi2
        tXi2s = E("scalar", lambda s: s.activation(mv[:, 6:7], mv[:, 5:6], AF.Sqrt), waits=(tXi2,), inc=ssem)
        def h2fn(v):
            VF(v, v.reciprocal(mv[:, 6:7], mv[:, 6:7]))
            return v.tensor_scalar(ht[:, :], x1t[:, :], mv[:, 6:7], 0.0, op0=ALU.mult, op1=ALU.add)
        tH2 = E("vector", h2fn, waits=(tXi2s, last.get("ht_free")), inc=vsem)
        tSxi2 = E("scalar", lambda s, t=t: s.dma_start(out=xi2d[128 * t:128 * (t + 1), :], in_=x1t[:, :]),
                  waits=(tXi2,), inc=ost, inc_by=16)
        last["x1t_free"] = tSxi2
        if t >= 1:
            for ch in range(8):
                tT2 = E("sync", lambda sy, t=t, ch=ch: sy.dma_start_transpose(
                    h2T[:, 2048 * ch + 128 * (t - 1):2048 * ch + 128 * (t - 1) + 128],
                    ht[:, 128 * ch:128 * (ch + 1)]),
                    waits=(tH2,) if ch == 0 else (), inc=txh2, inc_by=16)
            last["ht_free"] = tT2
            tHTw[t] = tT2
        else:
            last["ht_free"] = tH2
            tHTw[t] = tH2

    def h2wfn(v):
        for ch in range(8):
            _i = v.tensor_scalar(h2T[:, 2048 * ch:2048 * (ch + 1)], h2T[:, 2048 * ch:2048 * (ch + 1)],
                                 fnws[:, ch:ch + 1], 0.0, op0=ALU.mult, op1=ALU.add)
        return _i
    tH2W = E("vector", h2wfn, waits=(tHTw[NT - 1],), inc=vsem)

    tComb = [None] * NT
    for t in range(1, NT):
        def gfn(te, t=t):
            for ch in range(8):
                te.matmul(pS[:, 0:32], h2T[:, 2048 * ch + 128 * (t - 1):2048 * ch + 128 * (t - 1) + 128],
                          gatew[:, 32 * ch:32 * (ch + 1)], start=(ch == 0), stop=False)
            return te.matmul(pS[:, 0:32], onesr[:, :], gateb[:, :], start=False, stop=True)
        wg = [tH2W] if t == 1 else [last.get("pSg_free")]
        tG = E("tensor", gfn, waits=tuple(w for w in wg if w), inc=tsem)
        def routefn(v, t=t):
            VF(v, v.tensor_copy(glog[:, :], pS[:, 0:32]))
            for k in range(4):
                VF(v, v.reduce_max(tv[:, k:k + 1], glog[:, :], axis=AX))
                VF(v, v.tensor_scalar(gmask[:, 32 * k:32 * (k + 1)], glog[:, :], tv[:, k:k + 1], 1.0,
                                      op0=ALU.is_ge, op1=ALU.mult))
                if k < 3:
                    VF(v, v.scalar_tensor_tensor(glog[:, :], gmask[:, 32 * k:32 * (k + 1)], NEG,
                                                 glog[:, :], op0=ALU.mult, op1=ALU.add))
            return v.tensor_scalar(tv[:, 4:8], tv[:, 0:4], tv[:, 0:1], 0.0, op0=ALU.subtract, op1=ALU.add)
        tRt1 = E("vector", routefn, waits=(tG,), inc=vsem)
        tRt2 = E("scalar", lambda s: s.activation(tv[:, 8:12], tv[:, 4:8], AF.Exp), waits=(tRt1,), inc=ssem)
        def route2fn(v, t=t):
            VF(v, v.tensor_reduce(tv[:, 12:13], tv[:, 8:12], axis=AX, op=ALU.add))
            VF(v, v.reciprocal(tv[:, 12:13], tv[:, 12:13]))
            VF(v, v.tensor_scalar(tv[:, 8:12], tv[:, 8:12], tv[:, 12:13], 0.0, op0=ALU.mult, op1=ALU.add))
            cslice = comb[:, 32 * (t - 1):32 * t]
            VF(v, v.tensor_scalar(cslice, gmask[:, 0:32], tv[:, 8:9], 0.0, op0=ALU.mult, op1=ALU.add))
            for k in range(1, 4):
                _i = v.scalar_tensor_tensor(cslice, gmask[:, 32 * k:32 * (k + 1)], tv[:, 8 + k:9 + k],
                                            cslice, op0=ALU.mult, op1=ALU.add)
                if k < 3:
                    VF(v, _i)
            return _i
        tComb[t] = E("vector", route2fn, waits=(tRt2,), inc=vsem)
        last["pSg_free"] = tComb[t]

    if P.dbg is not None:
        E("scalar", lambda s: s.dma_start(out=P.dbg["d_comb"][:, :], in_=comb[:, :]),
          waits=(tComb[16],), inc=osem, inc_by=16)
        E("scalar", lambda s: s.dma_start(out=P.dbg["d_h2T"][:, :], in_=h2T[:, 0:2048]),
          waits=(tH2W,), inc=osem, inc_by=16)
    wbuf_t = [None, None]
    for tc in range(4):
        tzero = E("vector", lambda v, tc=tc: v.memset(macc[:, :], 0.0),
                  waits=(last.get("macc_free"),), inc=vsem)
        for e in range(32):
            tWl = [None, None]
            for ffh in range(2):
                buf = (2 * e + ffh) % 2
                ldu = ldu0 if buf == 0 else ldu1
                def wldfn(g, e=e, ffh=ffh, buf=buf, ldu=ldu):
                    for ch in range(8):
                        g.dma_start(out=wup[:, 8192 * buf + 1024 * ch:8192 * buf + 1024 * ch + 1024],
                                    in_=ins["Wup"][e, 128 * ch:128 * (ch + 1), 1024 * ffh:1024 * (ffh + 1)]
                                    ).then_inc(ldu.h, 16)
                    return None
                E("gpsimd", wldfn, waits=(wbuf_t[buf],))
                ldu.n += 16 * 8
                tWl[ffh] = ldu.ticket()
            def wdfn(g, e=e):
                for ch in range(8):
                    g.dma_start(out=wdn[:, 1024 * ch:1024 * (ch + 1)],
                                in_=ins["Wdn"][e, 128 * ch:128 * (ch + 1), :]).then_inc(ldd.h, 16)
                return None
            E("gpsimd", wdfn, waits=(last.get("wdn_free"),))
            ldd.n += 16 * 8
            tWd = ldd.ticket()
            tU = [None] * 8
            for ffh in range(2):
                buf = (2 * e + ffh) % 2
                for fft in range(8):
                    pb = pQ if (fft % 2 == 0) else pQ2
                    def upfn(te, e=e, buf=buf, ffh=ffh, fft=fft, tc=tc, pb=pb):
                        tok0 = 512 * tc
                        for ch in range(8):
                            _i = te.matmul(pb[:, :],
                                           wup[:, 8192 * buf + 1024 * ch + 128 * fft:
                                               8192 * buf + 1024 * ch + 128 * fft + 128],
                                           h2T[:, 2048 * ch + tok0:2048 * ch + tok0 + 512],
                                           start=(ch == 0), stop=(ch == 7))
                        return _i
                    wu = [tWl[ffh], tH2W if (e == 0 and tc == 0) else None,
                          last.get(f"pU{fft % 2}_free")] if fft == 0 else [
                        last.get(f"pU{fft % 2}_free")]
                    tUp = E("tensor", upfn, waits=tuple(w for w in wu if w), inc=tsem)
                    if fft == 7:
                        wbuf_t[buf] = tUp
                    if ffh == 0:
                        def evu(v, fft=fft, pb=pb):
                            return v.tensor_copy(uT[:, 512 * fft:512 * (fft + 1)], pb[:, :])
                        tEv = E("vector", evu, waits=(tUp, tU[fft]), inc=vsem)
                    else:
                        def evg(s, fft=fft, pb=pb):
                            return s.activation(sT[:, 512 * fft:512 * (fft + 1)], pb[:, :], AF.Silu)
                        tEv = E("scalar", evg, waits=(tUp, tU[fft]), inc=ssem)
                    last[f"pU{fft % 2}_free"] = tEv
                    if ffh == 1:
                        def smul(v, fft=fft):
                            return v.tensor_tensor(sT[:, 512 * fft:512 * (fft + 1)],
                                                   sT[:, 512 * fft:512 * (fft + 1)],
                                                   uT[:, 512 * fft:512 * (fft + 1)], op=ALU.mult)
                        tU[fft] = E("vector", smul, waits=(tEv,), inc=vsem)
            tSfull = tU[7]
            tDn = [None, None]
            for ts_ in range(4):
                for half in range(2):
                    pb = pP if half == 0 else pP2
                    def dnfn(te, e=e, ts_=ts_, half=half, pb=pb):
                        for ch in range(8):
                            _i = te.matmul(pb[:, :],
                                           sT[:, 512 * ch + 128 * ts_:512 * ch + 128 * ts_ + 128],
                                           wdn[:, 1024 * ch + 512 * half:1024 * ch + 512 * half + 512],
                                           start=(ch == 0), stop=(ch == 7))
                        return _i
                    wd = [tSfull if (ts_ == 0 and half == 0) else None, tWd if (ts_ == 0 and half == 0) else None,
                          last.get("pPd_free") if half == 0 else None]
                    tDn[half] = E("tensor", dnfn, waits=tuple(w for w in wd if w), inc=tsem)
                def accfn(v, e=e, tc=tc, ts_=ts_):
                    ti = 4 * tc + ts_
                    ce = comb[:, 32 * ti + e:32 * ti + e + 1]
                    VF(v, v.scalar_tensor_tensor(macc[:, 1024 * ts_:1024 * ts_ + 512], pP[:, :], ce,
                                                 macc[:, 1024 * ts_:1024 * ts_ + 512], op0=ALU.mult, op1=ALU.add))
                    return v.scalar_tensor_tensor(macc[:, 1024 * ts_ + 512:1024 * (ts_ + 1)], pP2[:, :], ce,
                                                  macc[:, 1024 * ts_ + 512:1024 * (ts_ + 1)], op0=ALU.mult, op1=ALU.add)
                wacc = [tDn[1], tzero if e == 0 else None, tComb[16] if (tc == 0 and e == 0) else None]
                tAcc = E("vector", accfn, waits=tuple(w for w in wacc if w), inc=vsem)
                last["pPd_free"] = tAcc
            last["wdn_free"] = tDn[1]
        for ts_ in range(4):
            ti = 4 * tc + ts_
            tLa = E("sync", lambda sy, ti=ti: sy.dma_start(out=xt[:, :], in_=x1d[128 * (ti + 1):128 * (ti + 2), :]),
                    waits=(last.get("xt_free"),), inc=ldx, inc_by=16)
            tLb = E("sync", lambda sy, ti=ti: sy.dma_start(out=at[:, :], in_=xi2d[128 * (ti + 1):128 * (ti + 2), :]),
                    waits=(last.get("at_free"),), inc=lda, inc_by=16)
            def outfn(v, ts_=ts_):
                v.tensor_tensor(outt[:, :], xt[:, :], at[:, :], op=ALU.add)
                return v.tensor_tensor(outt[:, :], outt[:, :], macc[:, 1024 * ts_:1024 * (ts_ + 1)], op=ALU.add)
            tOt = E("vector", outfn, waits=(tLa, tLb, tAcc, last.get("outt_free"), last.get("x1t_free")), inc=vsem)
            last["x1t_free"] = None
            last["xt_free"] = tOt
            last["at_free"] = tOt
            tSo = E("scalar", lambda s, ti=ti: s.dma_start(out=bout[128 * ti:128 * (ti + 1), :], in_=outt[:, :]),
                    waits=(tOt,), inc=osem, inc_by=16)
            last["outt_free"] = tSo
        if P.dbg is not None and tc == 3:
            E("scalar", lambda s: s.dma_start(out=P.dbg["d_macc"][:, :], in_=macc[:, :]),
              waits=(tAcc,), inc=osem, inc_by=16)
        last["macc_free"] = tOt


"""L1_attn launch: j-sharded chunked delta rule (DeltaNet-style).

Math (per core, owning output dims J = 128 cols):
  v = x @ WkvT_j + bkv_j        (x = LN(x2), all 16384 rows)
  chunks of C=64 timesteps -> 128 rows ordered [b0 t-block 64 | b1 t-block 64]
  per chunk c: A = X_c X_c^T ; L = eta' * strictmask(A)  (eta' = eta*2/2048)
    R' = X_c W^T - V_c
    E = (I-L)(I+L^2)(I+L^4)(I+L^8)(I+L^16)(I+L^32) R'
    Y_c = V_c + E ;  W^T -= eta' * X_c^T E
Inputs (host layout):
  XT    [1024, 16384] f32  : xln2 feature-major, columns PERMUTED chunk-order
                             (col = 128*c + 64*half + tl, token = half*8192 + 64*c + tl)
  X     [16384, 1024] f32  : same rows, row-major (chunk-order rows)
  W0T   [1024, 128]  f32   : W0[jblock, :].T
  WkvT  [1024, 128]  f32   : Wkv[1024+jblock, :].T
  bkvb  [1, 128]     f32
  eta   [1, 1]       f32
  maskT [128, 128]   f32   : maskT[i,j] = 1 if t(j) > t(i) else 0  ( = L^T pattern: strict upper in t)
  maskL [128, 128]   f32   : mask[i,j] = 1 if t(i) > t(j) else 0
Output:
  yj    [16384, 128] f32   : rows in the SAME permuted chunk order
"""
import numpy as np

NCH = 128
ETA_SCALE = 2.0 / 2048.0


def build_l1_attn(P):
    ins = {}
    for name, shape in [("XT", [1024, NCH * 128]), ("X", [NCH * 128, 1024]),
                        ("W0T", [1024, 128]), ("WkvT", [1024, 128]),
                        ("bkvb", [1, 128]), ("eta", [1, 1]),
                        ("maskT", [128, 128]), ("maskL", [128, 128])]:
        ins[name] = P.dram(name, shape, F32, kind="ExternalInput")
    yj = P.dram("yj", [NCH * 128, 128], F32, kind="ExternalOutput")
    emit_l1_attn(P, ins, yj)
    osem = P.sem("c_osem")
    P.barrier_wait("scalar", osem.ticket())
    return P


def emit_l1_attn(P, ins, yj, tag="c"):
    S = lambda n: tag + "_" + n
    E = P.emit

    Wm   = P.sbuf(S("Wm"), [128, 8 * 128], F32)
    Wb   = P.sbuf(S("Wb"), [128, 8 * 128], BF16)
    Wkv  = P.sbuf(S("Wkv"), [128, 8 * 128], BF16)
    bkvb = P.sbuf(S("bkvb"), [1, 128], BF16)
    etac = P.sbuf(S("etac"), [128, 2], F32)
    etaS = P.sbuf(S("etaS"), [1, 2], F32)
    Vb   = P.sbuf(S("Vb"), [128, NCH * 128], BF16)
    XTb  = P.sbuf(S("XTb"), [128, 2 * 8 * 128], BF16)
    Xrb  = P.sbuf(S("Xrb"), [128, 2 * 1024], BF16)
    maskT = P.sbuf(S("maskT"), [128, 128], BF16)
    maskL = P.sbuf(S("maskL"), [128, 128], BF16)
    LT   = P.sbuf(S("LT"), [128, 6 * 128], BF16)
    Lf   = P.sbuf(S("Lf"), [128, 6 * 128], BF16)
    Eb   = P.sbuf(S("Eb"), [128, 128], BF16)
    tb   = P.sbuf(S("tb"), [128, 128], F32)
    yb   = P.sbuf(S("yb"), [128, 128], F32)
    onesrf = P.sbuf(S("onesrf"), [1, 128], F32)
    onesrb = P.sbuf(S("onesrb"), [1, 128], BF16)
    scr  = P.sbuf(S("scr"), [128, 128], F32)

    pA  = P.psum(S("pA"), [128, 512], F32)
    pB  = P.psum(S("pB"), [128, 512], F32)
    pW  = P.psum(S("pW"), [128, 512], F32)
    pW2 = P.psum(S("pW2"), [128, 512], F32)

    ld   = P.sem(S("ld"))
    ldx0 = P.sem(S("ldx0"))
    ldx1 = P.sem(S("ldx1"))
    ldr0 = P.sem(S("ldr0"))
    ldr1 = P.sem(S("ldr1"))
    ldw  = P.sem(S("ldw"))
    tsem = P.sem(S("tsem"))
    vsem = P.sem(S("vsem"))
    ssem = P.sem(S("ssem"))
    osem = P.sem(S("osem"))
    vf   = P.sem(S("vf"))
    vfc = {"n": 0}
    def VF(v, instr):
        if P.paranoid:
            return
        vfc["n"] += 1
        instr.then_inc(vf.h, 1)
        v.wait_ge(vf.h, vfc["n"])

    def gl(g):
        for ch in range(8):
            g.dma_start(out=Wkv[:, 128 * ch:128 * (ch + 1)],
                        in_=ins["WkvT"][128 * ch:128 * (ch + 1), :]).then_inc(ldw.h, 16)
        g.dma_start(out=bkvb[:, :], in_=ins["bkvb"][:, :]).then_inc(ldw.h, 16)
        g.dma_start(out=maskT[:, :], in_=ins["maskT"][:, :]).then_inc(ldw.h, 16)
        g.dma_start(out=maskL[:, :], in_=ins["maskL"][:, :]).then_inc(ldw.h, 16)
        return None
    E("gpsimd", gl)
    ldw.n += 16 * 11
    tGl = ldw.ticket()
    def sl(sy):
        for ch in range(8):
            sy.dma_start(out=Wm[:, 128 * ch:128 * (ch + 1)],
                         in_=ins["W0T"][128 * ch:128 * (ch + 1), :]).then_inc(ld.h, 16)
        sy.dma_start(out=etaS[:, 0:1], in_=ins["eta"][:, :]).then_inc(ld.h, 16)
        return None
    E("sync", sl)
    ld.n += 16 * 9
    tW0 = ld.ticket()
    E("vector", lambda v: v.memset(onesrf[:], 1.0))
    E("vector", lambda v: v.memset(onesrb[:], 1.0), inc=vsem)
    tOnes = vsem.ticket()
    def etafn(v):
        VF(v, v.tensor_scalar(etaS[:, 0:1], etaS[:, 0:1], ETA_SCALE, 0.0, op0=ALU.mult, op1=ALU.add))
        return v.tensor_scalar(etaS[:, 1:2], etaS[:, 0:1], -1.0, 0.0, op0=ALU.mult, op1=ALU.add)
    tEta = E("vector", etafn, waits=(tW0,), inc=vsem)
    tEtaR = E("tensor", lambda te: te.matmul(pA[:, 0:2], onesrf[:, :], etaS[:, 0:2],
                                             start=True, stop=True), waits=(tEta, tOnes), inc=tsem)
    tEtaC = E("vector", lambda v: v.tensor_copy(etac[:, :], pA[:, 0:2]), waits=(tEtaR,), inc=vsem)
    tWb0 = E("vector", lambda v: v.tensor_copy(Wb[:, :], Wm[:, :]), waits=(tW0,), inc=vsem)

    last = {}

    tVdone = None
    for c in range(NCH):
        buf = c % 2
        ldx = ldx0 if buf == 0 else ldx1
        def xlfn(g, c=c, buf=buf, ldx=ldx):
            for ch in range(8):
                g.dma_start(out=XTb[:, 1024 * buf + 128 * ch:1024 * buf + 128 * ch + 128],
                            in_=ins["XT"][128 * ch:128 * (ch + 1), 128 * c:128 * (c + 1)]).then_inc(ldx.h, 16)
            return None
        E("gpsimd", xlfn, waits=(last.get(f"XT{buf}_free"),))
        ldx.n += 16 * 8
        tXL = ldx.ticket()
        def vfn(te, c=c, buf=buf):
            for ch in range(8):
                te.matmul(pA[:, 0:128], XTb[:, 1024 * buf + 128 * ch:1024 * buf + 128 * ch + 128],
                          Wkv[:, 128 * ch:128 * (ch + 1)], start=(ch == 0), stop=False)
            return te.matmul(pA[:, 0:128], onesrb[:, :], bkvb[:, :], start=False, stop=True)
        wv = [tXL, tGl if c == 0 else None, tOnes if c == 0 else None,
              tEtaC if c == 0 else None, last.get("pAv_free")]
        tV = E("tensor", vfn, waits=tuple(w for w in wv if w), inc=tsem)
        tVe = E("vector", lambda v, c=c: v.tensor_copy(Vb[:, 128 * c:128 * (c + 1)], pA[:, 0:128]),
                waits=(tV,), inc=vsem)
        last[f"XT{buf}_free"] = tV
        last["pAv_free"] = tVe
        tVdone = tVe

    for c in range(NCH):
        buf = c % 2
        ldx = ldx0 if buf == 0 else ldx1
        ldr = ldr0 if buf == 0 else ldr1
        def xlfn2(g, c=c, buf=buf, ldx=ldx):
            for ch in range(8):
                g.dma_start(out=XTb[:, 1024 * buf + 128 * ch:1024 * buf + 128 * ch + 128],
                            in_=ins["XT"][128 * ch:128 * (ch + 1), 128 * c:128 * (c + 1)]).then_inc(ldx.h, 16)
            return None
        E("gpsimd", xlfn2, waits=(last.get(f"XT{buf}_free"),))
        ldx.n += 16 * 8
        tXL = ldx.ticket()
        tXR = E("gpsimd", lambda g, c=c, buf=buf: g.dma_start(
            out=Xrb[:, 1024 * buf:1024 * (buf + 1)], in_=ins["X"][128 * c:128 * (c + 1), :]),
            waits=(last.get(f"XR{buf}_free"),), inc=ldr, inc_by=16)

        def afn(te, c=c, buf=buf):
            for ch in range(8):
                _i = te.matmul(pA[:, 128:256], XTb[:, 1024 * buf + 128 * ch:1024 * buf + 128 * ch + 128],
                               XTb[:, 1024 * buf + 128 * ch:1024 * buf + 128 * ch + 128],
                               start=(ch == 0), stop=(ch == 7))
            return _i
        wa = [tXL, last.get("pAa_free")]
        tA = E("tensor", afn, waits=tuple(w for w in wa if w), inc=tsem)
        def lfn(v, c=c):
            v.tensor_tensor(scr[:, :], pA[:, 128:256], maskT[:, :], op=ALU.mult)
            v.tensor_scalar(LT[:, 0:128], scr[:, :], etac[:, 0:1], 0.0, op0=ALU.mult, op1=ALU.add)
            v.tensor_tensor(scr[:, :], pA[:, 128:256], maskL[:, :], op=ALU.mult)
            return v.tensor_scalar(Lf[:, 0:128], scr[:, :], etac[:, 0:1], 0.0, op0=ALU.mult, op1=ALU.add)
        wl = [tA, tEtaC if c == 0 else None, tGl if c == 0 else None, last.get("LT_free")]
        tL = E("vector", lfn, waits=tuple(w for w in wl if w), inc=vsem)
        last["pAa_free"] = tL
        tSqV = tL
        for lev in range(1, 6):
            def sqfn(te, lev=lev):
                te.matmul(pB[:, 0:128], LT[:, 128 * (lev - 1):128 * lev],
                          Lf[:, 128 * (lev - 1):128 * lev], start=True, stop=True)
                return te.matmul(pB[:, 128:256], Lf[:, 128 * (lev - 1):128 * lev],
                                 LT[:, 128 * (lev - 1):128 * lev], start=True, stop=True)
            tSq = E("tensor", sqfn, waits=(tSqV, last.get("pB_free")), inc=tsem)
            def sqev(v, lev=lev):
                v.tensor_copy(Lf[:, 128 * lev:128 * (lev + 1)], pB[:, 0:128])
                return v.tensor_copy(LT[:, 128 * lev:128 * (lev + 1)], pB[:, 128:256])
            tSqV = E("vector", sqev, waits=(tSq,), inc=vsem)
            last["pB_free"] = tSqV

        def rfn(te, c=c, buf=buf):
            for ch in range(8):
                _i = te.matmul(pA[:, 256:384], XTb[:, 1024 * buf + 128 * ch:1024 * buf + 128 * ch + 128],
                               Wb[:, 128 * ch:128 * (ch + 1)], start=(ch == 0), stop=(ch == 7))
            return _i
        wr = [tXL, tWb0 if c == 0 else None, last.get("pAr_free"), last.get("Wb_free")]
        tR = E("tensor", rfn, waits=tuple(w for w in wr if w), inc=tsem)
        last[f"XT{buf}_free"] = tR
        tRp = E("vector", lambda v, c=c: v.tensor_tensor(tb[:, :], pA[:, 256:384],
                                                         Vb[:, 128 * c:128 * (c + 1)], op=ALU.subtract),
                waits=(tR, tVdone if c == 0 else None, last.get("tb_free")), inc=vsem)
        last["pAr_free"] = tRp

        tcur = tRp
        for step, (lev, sign) in enumerate([(5, 1.0), (4, 1.0), (3, 1.0), (2, 1.0), (1, 1.0), (0, -1.0)]):
            def solfn(te, lev=lev):
                return te.matmul(pB[:, 256:384], LT[:, 128 * lev:128 * (lev + 1)], Eb[:, :],
                                 start=True, stop=True)
            tcast = E("vector", lambda v: v.tensor_copy(Eb[:, :], tb[:, :]),
                      waits=(tcur, last.get("Eb_free") if step == 0 else None), inc=vsem)
            tMM = E("tensor", solfn, waits=(tcast, last.get("pBs_free")), inc=tsem)
            def upfn(v, sign=sign):
                if sign > 0:
                    return v.tensor_tensor(tb[:, :], tb[:, :], pB[:, 256:384], op=ALU.add)
                return v.tensor_tensor(tb[:, :], tb[:, :], pB[:, 256:384], op=ALU.subtract)
            tcur = E("vector", upfn, waits=(tMM,), inc=vsem)
            last["pBs_free"] = tcur
        tEc = E("vector", lambda v: v.tensor_copy(Eb[:, :], tb[:, :]), waits=(tcur,), inc=vsem)

        def yfn(v, c=c):
            return v.tensor_tensor(yb[:, :], tb[:, :], Vb[:, 128 * c:128 * (c + 1)], op=ALU.add)
        tY = E("vector", yfn, waits=(tcur, last.get("yb_free")), inc=vsem)
        last["tb_free"] = tY
        tSo = E("scalar", lambda s, c=c: s.dma_start(out=yj[128 * c:128 * (c + 1), :], in_=yb[:, :]),
                waits=(tY,), inc=osem, inc_by=16)
        last["yb_free"] = tSo

        tDW = [None, None]
        for half in range(2):
            pw = pW if half == 0 else pW2
            def dwfn(te, half=half, buf=buf, pw=pw):
                for dt_ in range(4):
                    _i = te.matmul(pw[:, 128 * dt_:128 * (dt_ + 1)],
                                   Xrb[:, 1024 * buf + 512 * half + 128 * dt_:
                                       1024 * buf + 512 * half + 128 * dt_ + 128],
                                   Eb[:, :], start=True, stop=True)
                return _i
            wd = [tEc, tXR, last.get(f"pW{half}_free")]
            tDW[half] = E("tensor", dwfn, waits=tuple(w for w in wd if w), inc=tsem)
        last[f"XR{buf}_free"] = tDW[1]
        def wupfn(v, buf=buf):
            for half in range(2):
                pw = pW if half == 0 else pW2
                for dt_ in range(4):
                    ch = 4 * half + dt_
                    v.scalar_tensor_tensor(Wm[:, 128 * ch:128 * (ch + 1)],
                                           pw[:, 128 * dt_:128 * (dt_ + 1)], etac[:, 1:2],
                                           Wm[:, 128 * ch:128 * (ch + 1)], op0=ALU.mult, op1=ALU.add)
            return v.tensor_copy(Wb[:, :], Wm[:, :])
        tWu = E("vector", wupfn, waits=(tDW[0], tDW[1]), inc=vsem)
        last["pW0_free"] = tWu
        last["pW1_free"] = tWu
        last["LT_free"] = tMM
        last["Eb_free"] = tDW[1]
        last["Wb_free"] = tWu


"""L1_ffn launches (fake-quant SwiGLU). Three stages because of the two
global max-reductions:
  d1: x3 = x2 + y3 ; xln = LN(x3) ; out lmax = max|xln| (local)
  d2: global max -> sx ; xq ints ; per-i-tile: quantize w_gate/w_up rows,
      transpose ints, gate/up matmuls, h = silu(g)*u (scaled), local max|h|
      out: h [2816, 2048] bf16 (feature-major), lmaxh
  d3: global maxh -> sh ; hq ints ; quantize w_down ; down matmul ;
      out = x3 + down  -> dout [2048, 1024] f32
Round: one-sided floor chain (exact in fp32 HW and fp64 shadow):
  q = clip(round(u)) with u pre-clipped to [-130, 130]:
  w = u + 130.5 ; y = frac(w) via 9 is_ge stages ; q = (w - 130) - y ; clip.
"""
import numpy as np

QMAX = 127.0


def _round_chain(v, VF, out, u, scr, cmp_):
    v.tensor_scalar(u, u, 130.0, -130.0, op0=ALU.min, op1=ALU.max)
    v.tensor_scalar(scr, u, 130.5, 0.0, op0=ALU.add, op1=ALU.add)
    v.tensor_copy(out, scr)
    for p in [256.0, 128.0, 64.0, 32.0, 16.0, 8.0, 4.0, 2.0, 1.0]:
        v.tensor_scalar(cmp_, out, p, -p, op0=ALU.is_ge, op1=ALU.mult)
        v.tensor_tensor(out, out, cmp_, op=ALU.add)
    i = v.scalar_tensor_tensor(out, scr, -130.0, out, op0=ALU.add, op1=ALU.subtract)
    return i


def _ln_tile(v, VF, AFx, E, P, xg, sq, mv, c0):
    m = mv[:, c0:c0 + 1]
    VF(v, v.tensor_reduce(m, xg, axis=AX, op=ALU.add))
    VF(v, v.tensor_scalar(m, m, 1.0 / 1024.0, 0.0, op0=ALU.mult, op1=ALU.add))
    v.tensor_scalar(xg, xg, m, 0.0, op0=ALU.subtract, op1=ALU.add)
    v.tensor_tensor(sq, xg, xg, op=ALU.mult)
    VF(v, v.tensor_reduce(mv[:, c0 + 1:c0 + 2], sq, axis=AX, op=ALU.add))
    return v.tensor_scalar(mv[:, c0 + 1:c0 + 2], mv[:, c0 + 1:c0 + 2],
                           1.0 / 1024.0, 1e-5, op0=ALU.mult, op1=ALU.add)


def build_d1(P):
    x2r = P.dram("x2r", [2048, 1024], F32, kind="ExternalInput")
    y3r = P.dram("y3r", [2048, 1024], F32, kind="ExternalInput")
    lmax = P.dram("lmax", [1, 1], F32, kind="ExternalOutput")
    S = lambda n: "d1_" + n
    E = P.emit
    xt = P.sbuf(S("xt"), [128, 1024], F32)
    yt = P.sbuf(S("yt"), [128, 1024], F32)
    sq = P.sbuf(S("sq"), [128, 1024], F32)
    mv = P.sbuf(S("mv"), [128, 4], F32)
    mx = P.sbuf(S("mx"), [128, 2], F32)
    mxo = P.sbuf(S("mxo"), [128, 1], F32)
    ld = P.sem(S("ld")); ld2 = P.sem(S("ld2")); vsem = P.sem(S("vsem"))
    ssem = P.sem(S("ssem")); osem = P.sem(S("osem")); vf = P.sem(S("vf"))
    vfc = {"n": 0}
    def VF(v, instr):
        if P.paranoid: return
        vfc["n"] += 1; instr.then_inc(vf.h, 1); v.wait_ge(vf.h, vfc["n"])
    last = {}
    for t in range(16):
        tL = E("sync", lambda sy, t=t: sy.dma_start(out=xt[:], in_=x2r[128 * t:128 * (t + 1), :]),
               waits=(last.get("xt_free"),), inc=ld, inc_by=16)
        tL2 = E("sync", lambda sy, t=t: sy.dma_start(out=yt[:], in_=y3r[128 * t:128 * (t + 1), :]),
                waits=(last.get("yt_free"),), inc=ld2, inc_by=16)
        def f1(v, t=t):
            v.tensor_tensor(xt[:], xt[:], yt[:], op=ALU.add)
            _ln_tile(v, VF, AF, E, P, xt[:], sq[:], mv, 0)
            return v.nop() if False else v.tensor_scalar(mv[:, 2:3], mv[:, 1:2], 1.0, 0.0, op0=ALU.mult, op1=ALU.add)
        tF = E("vector", f1, waits=(tL, tL2, last.get("busy")), inc=vsem)
        tS = E("scalar", lambda s: s.activation(mv[:, 2:3], mv[:, 1:2], AF.Sqrt), waits=(tF,), inc=ssem)
        def f2(v, t=t):
            VF(v, v.reciprocal(mv[:, 2:3], mv[:, 2:3]))
            v.tensor_scalar(xt[:], xt[:], mv[:, 2:3], 0.0, op0=ALU.mult, op1=ALU.add)
            VF(v, v.tensor_reduce(mx[:, 0:1], xt[:], axis=AX, op=ALU.max, apply_absolute_value=True))
            if t == 0:
                return v.tensor_copy(mx[:, 1:2], mx[:, 0:1])
            return v.tensor_tensor(mx[:, 1:2], mx[:, 1:2], mx[:, 0:1], op=ALU.max)
        tF2 = E("vector", f2, waits=(tS,), inc=vsem)
        last["xt_free"] = tF2
        last["yt_free"] = tF
        last["busy"] = tF2
    mrow = P.sbuf(S("mrow"), [1, 128], F32)
    gsem = P.sem(S("gsem"))
    tPR = E("gpsimd", lambda g: g.dma_start(out=mrow[0:1, :], in_=mx[:, 1:2]),
            waits=(last["busy"],), inc=gsem, inc_by=16)
    def finmax(v):
        VF(v, v.tensor_reduce(mxo[0:1, 0:1], mrow[0:1, :], axis=AX, op=ALU.max))
        return v.tensor_copy(mxo[0:1, 0:1], mxo[0:1, 0:1])
    tFm = E("vector", finmax, waits=(tPR,), inc=vsem)
    tOut = E("scalar", lambda s: s.dma_start(out=lmax[:, :], in_=mxo[0:1, 0:1]),
             waits=(tFm,), inc=osem, inc_by=16)
    P.barrier_wait("scalar", osem.ticket())
    return P


def build_d2(P):
    x2r = P.dram("x2r", [2048, 1024], F32, kind="ExternalInput")
    y3r = P.dram("y3r", [2048, 1024], F32, kind="ExternalInput")
    lmax8 = P.dram("lmax8", [1, 8], F32, kind="ExternalInput")
    clipx = P.dram("clipx", [1, 1], F32, kind="ExternalInput")
    wg = P.dram("wg", [2816, 1024], F32, kind="ExternalInput")
    wu = P.dram("wu", [2816, 1024], F32, kind="ExternalInput")
    hout = P.dram("hout", [2816, 2048], BF16, kind="ExternalOutput")
    lmaxh = P.dram("lmaxh", [1, 1], F32, kind="ExternalOutput")
    S = lambda n: "d2_" + n
    E = P.emit
    xt = P.sbuf(S("xt"), [128, 1024], F32)
    yt = P.sbuf(S("yt"), [128, 1024], F32)
    sq = P.sbuf(S("sq"), [128, 1024], F32)
    cmp_ = P.sbuf(S("cmp"), [128, 1024], F32)
    mv = P.sbuf(S("mv"), [128, 4], F32)
    xqt = P.sbuf(S("xqt"), [128, 1024], BF16)
    xqT = P.sbuf(S("xqT"), [128, 8 * 2048], BF16)
    sc1 = P.sbuf(S("sc1"), [1, 4], F32)
    rsxc = P.sbuf(S("rsxc"), [128, 2], F32)
    wt = P.sbuf(S("wt"), [128, 2 * 1024], F32)
    wqt = P.sbuf(S("wqt"), [128, 1024], BF16)
    wqTt = P.sbuf(S("wqTt"), [128, 8 * 128], BF16)
    sw = P.sbuf(S("sw"), [128, 4], F32)
    gbuf = P.sbuf(S("gbuf"), [128, 2048], F32)
    hbuf = P.sbuf(S("hbuf"), [128, 2048], BF16)
    mx = P.sbuf(S("mx"), [128, 2], F32)
    mxo = P.sbuf(S("mxo"), [128, 1], F32)
    onesrf = P.sbuf(S("onesrf"), [1, 128], F32)
    pG = P.psum(S("pG"), [128, 512], F32)
    pU = P.psum(S("pU"), [128, 512], F32)
    pR = P.psum(S("pR"), [128, 512], F32)
    ld = P.sem(S("ld")); ld2 = P.sem(S("ld2")); ldw = P.sem(S("ldw"))
    ldw2 = P.sem(S("ldw2"))
    tx = P.sem(S("tx")); txw = P.sem(S("txw"))
    tsem = P.sem(S("tsem")); vsem = P.sem(S("vsem")); ssem = P.sem(S("ssem"))
    osem = P.sem(S("osem")); vf = P.sem(S("vf")); gsem = P.sem(S("gsem"))
    vfc = {"n": 0}
    def VF(v, instr):
        if P.paranoid: return
        vfc["n"] += 1; instr.then_inc(vf.h, 1); v.wait_ge(vf.h, vfc["n"])
    last = {}
    ldsc = P.sem(S("ldsc"))
    ldsc2 = P.sem(S("ldsc2"))
    tLm = E("sync", lambda sy: sy.dma_start(out=sc1[0:1, 0:1], in_=clipx[:, :]), inc=ldsc, inc_by=16)
    lm8 = P.sbuf(S("lm8"), [1, 8], F32)
    tL8 = E("sync", lambda sy: sy.dma_start(out=lm8[:, :], in_=lmax8[:, :]), inc=ldsc2, inc_by=16)
    E("vector", lambda v: v.memset(onesrf[:], 1.0), inc=vsem)
    tOnes = vsem.ticket()
    tSg = E("scalar", lambda s: s.activation(sc1[:, 3:4], sc1[:, 0:1], AF.Sigmoid), waits=(tLm,), inc=ssem)
    def sxfn(v):
        VF(v, v.tensor_reduce(sc1[:, 0:1], lm8[:, :], axis=AX, op=ALU.max))
        VF(v, v.tensor_tensor(sc1[:, 1:2], sc1[:, 3:4], sc1[:, 0:1], op=ALU.mult))
        VF(v, v.tensor_scalar(sc1[:, 1:2], sc1[:, 1:2], 1.0 / QMAX, 0.0, op0=ALU.mult, op1=ALU.add))
        return v.reciprocal(sc1[:, 2:3], sc1[:, 1:2])
    tSx = E("vector", sxfn, waits=(tL8, tSg), inc=vsem)
    tRep = E("tensor", lambda te: te.matmul(pR[:, 0:2], onesrf[:, :], sc1[:, 1:3], start=True, stop=True),
             waits=(tSx, tOnes), inc=tsem)
    tRsx = E("vector", lambda v: v.tensor_copy(rsxc[:, :], pR[:, 0:2]), waits=(tRep,), inc=vsem)

    for t in range(16):
        tL = E("sync", lambda sy, t=t: sy.dma_start(out=xt[:], in_=x2r[128 * t:128 * (t + 1), :]),
               waits=(last.get("xt_free"),), inc=ld, inc_by=16)
        tL2 = E("sync", lambda sy, t=t: sy.dma_start(out=yt[:], in_=y3r[128 * t:128 * (t + 1), :]),
                waits=(last.get("yt_free"),), inc=ld2, inc_by=16)
        def f1(v, t=t):
            v.tensor_tensor(xt[:], xt[:], yt[:], op=ALU.add)
            return _ln_tile(v, VF, AF, E, P, xt[:], sq[:], mv, 0)
        tF = E("vector", f1, waits=(tL, tL2, last.get("busy")), inc=vsem)
        tS = E("scalar", lambda s: s.activation(mv[:, 2:3], mv[:, 1:2], AF.Sqrt), waits=(tF,), inc=ssem)
        def f2(v, t=t):
            VF(v, v.reciprocal(mv[:, 2:3], mv[:, 2:3]))
            v.tensor_scalar(xt[:], xt[:], mv[:, 2:3], 0.0, op0=ALU.mult, op1=ALU.add)
            v.tensor_scalar(xt[:], xt[:], rsxc[:, 1:2], 0.0, op0=ALU.mult, op1=ALU.add)
            _round_chain(v, VF, sq[:], xt[:], yt[:], cmp_[:])
            return v.tensor_scalar(xqt[:], sq[:], QMAX, -QMAX - 1.0, op0=ALU.min, op1=ALU.max)
        tF2 = E("vector", f2, waits=(tS, tRsx if t == 0 else None), inc=vsem)
        last["xt_free"] = tF2
        last["yt_free"] = tF2
        last["busy"] = tF2
        for ch in range(8):
            tT = E("sync", lambda sy, t=t, ch=ch: sy.dma_start_transpose(
                xqT[:, 2048 * ch + 128 * t:2048 * ch + 128 * t + 128],
                xqt[:, 128 * ch:128 * (ch + 1)]),
                waits=(tF2,) if ch == 0 else (), inc=tx, inc_by=16)
        last["busy"] = tT
    tXQ = tT

    for it in range(22):
        tGe = [None] * 4
        tHacc = None
        for wi, wsrc in ((0, wg), (1, wu)):
            buf = (2 * it + wi) % 2
            tWL = E("gpsimd", lambda g, it=it, wsrc=wsrc, buf=buf: g.dma_start(
                out=wt[:, 1024 * buf:1024 * (buf + 1)], in_=wsrc[128 * it:128 * (it + 1), :]),
                waits=(last.get(f"wt{buf}_free"),), inc=(ldw if buf == 0 else ldw2), inc_by=16)
            def qfn(v, buf=buf, wi=wi):
                wtile = wt[:, 1024 * buf:1024 * (buf + 1)]
                VF(v, v.tensor_reduce(sw[:, 0:1], wtile, axis=AX, op=ALU.max, apply_absolute_value=True))
                VF(v, v.tensor_scalar(sw[:, 0:1], sw[:, 0:1], 1.0 / QMAX, 0.0, op0=ALU.mult, op1=ALU.add))
                VF(v, v.reciprocal(sw[:, 1:2], sw[:, 0:1]))
                v.tensor_scalar(sq[:], wtile, sw[:, 1:2], 0.0, op0=ALU.mult, op1=ALU.add)
                _round_chain(v, VF, cmp_[:], sq[:], xt[:], yt[:])
                VF(v, v.tensor_scalar(wqt[:], cmp_[:], QMAX, -QMAX - 1.0, op0=ALU.min, op1=ALU.max))
                return v.tensor_tensor(sw[:, 2 + wi:3 + wi], sw[:, 0:1], rsxc[:, 0:1], op=ALU.mult)
            tQ = E("vector", qfn, waits=(tWL, tRsx if it == 0 else None, last.get("wq_free")), inc=vsem)
            for ch in range(8):
                tTW = E("sync", lambda sy, ch=ch: sy.dma_start_transpose(
                    wqTt[:, 128 * ch:128 * (ch + 1)], wqt[:, 128 * ch:128 * (ch + 1)]),
                    waits=(tQ, last.get("wqT_free")) if ch == 0 else (), inc=txw, inc_by=16)
            last["wq_free"] = tTW
            tMM = [None] * 4
            for tc in range(4):
                pb = pG if wi == 0 else pU
                def mfn(te, tc=tc, pb=pb):
                    for ch in range(8):
                        _i = te.matmul(pb[:, :], wqTt[:, 128 * ch:128 * (ch + 1)],
                                       xqT[:, 2048 * ch + 512 * tc:2048 * ch + 512 * tc + 512],
                                       start=(ch == 0), stop=(ch == 7))
                    return _i
                wm = [tTW if tc == 0 else None, tXQ if (it == 0 and wi == 0 and tc == 0) else None,
                      last.get(f"p{wi}_free")]
                tMM[tc] = E("tensor", mfn, waits=tuple(w for w in wm if w), inc=tsem)
                if wi == 0:
                    def gevac(s, tc=tc):
                        return s.activation(gbuf[:, 512 * tc:512 * (tc + 1)], pG[:, :],
                                            AF.Silu, scale=sw[:, 2:3])
                    tGe[tc] = E("scalar", gevac, waits=(tMM[tc],), inc=ssem)
                    last["p0_free"] = tGe[tc]
                else:
                    def uh(v, tc=tc):
                        sl_ = sq[:, 512 * (tc % 2):512 * (tc % 2) + 512]
                        v.tensor_scalar(sl_, pU[:, :], sw[:, 3:4], 0.0, op0=ALU.mult, op1=ALU.add)
                        return v.tensor_tensor(hbuf[:, 512 * tc:512 * (tc + 1)],
                                               gbuf[:, 512 * tc:512 * (tc + 1)], sl_, op=ALU.mult)
                    tUh = E("vector", uh, waits=(tMM[tc], tGe[tc], last.get("hbuf_free") if tc == 0 else None),
                            inc=vsem)
                    last["p1_free"] = tUh
                    tHacc = tUh
            last[f"wt{buf}_free"] = tQ
            last["wqT_free"] = tMM[3]
        def hmax(v, it=it):
            VF(v, v.tensor_reduce(mx[:, 0:1], hbuf[:, :], axis=AX, op=ALU.max, apply_absolute_value=True))
            if it == 0:
                return v.tensor_copy(mx[:, 1:2], mx[:, 0:1])
            return v.tensor_tensor(mx[:, 1:2], mx[:, 1:2], mx[:, 0:1], op=ALU.max)
        tHm = E("vector", hmax, waits=(tHacc,), inc=vsem)
        tHo = E("scalar", lambda s, it=it: s.dma_start(out=hout[128 * it:128 * (it + 1), :], in_=hbuf[:, :]),
                waits=(tHm,), inc=osem, inc_by=16)
        last["hbuf_free"] = tHo
    mrow = P.sbuf(S("mrow"), [1, 128], F32)
    tPR = E("gpsimd", lambda g: g.dma_start(out=mrow[0:1, :], in_=mx[:, 1:2]),
            waits=(tHm,), inc=gsem, inc_by=16)
    def finmax(v):
        VF(v, v.tensor_reduce(mxo[0:1, 0:1], mrow[0:1, :], axis=AX, op=ALU.max))
        return v.tensor_copy(mxo[0:1, 0:1], mxo[0:1, 0:1])
    tFm = E("vector", finmax, waits=(tPR,), inc=vsem)
    tOut = E("scalar", lambda s: s.dma_start(out=lmaxh[:, :], in_=mxo[0:1, 0:1]), waits=(tFm,),
             inc=osem, inc_by=16)
    P.barrier_wait("scalar", osem.ticket())
    return P


def build_d3(P):
    x2r = P.dram("x2r", [2048, 1024], F32, kind="ExternalInput")
    y3r = P.dram("y3r", [2048, 1024], F32, kind="ExternalInput")
    hin = P.dram("hin", [2816, 2048], BF16, kind="ExternalInput")
    lmaxh8 = P.dram("lmaxh8", [1, 8], F32, kind="ExternalInput")
    clipd = P.dram("clipd", [1, 1], F32, kind="ExternalInput")
    wd = P.dram("wd", [1024, 2816], F32, kind="ExternalInput")
    dout = P.dram("dout", [2048, 1024], F32, kind="ExternalOutput")
    S = lambda n: "d3_" + n
    E = P.emit
    wt  = P.sbuf(S("wt"), [128, 2 * 2816], F32)
    sq  = P.sbuf(S("sq"), [128, 2816], F32)
    scr = P.sbuf(S("scr"), [128, 2816], F32)
    cmp_ = P.sbuf(S("cmp"), [128, 2816], F32)
    wdq = P.sbuf(S("wdq"), [128, 2816], BF16)
    wdqT = P.sbuf(S("wdqT"), [128, 22 * 1024], BF16)
    sdc = P.sbuf(S("sdc"), [128, 10], F32)
    sc1 = P.sbuf(S("sc1"), [1, 4], F32)
    shc = P.sbuf(S("shc"), [128, 2], F32)
    lm8 = P.sbuf(S("lm8"), [1, 8], F32)
    hq  = P.sbuf(S("hq"), [128, 22 * 512], BF16)
    houT = P.sbuf(S("houT"), [128, 8 * 512], BF16)
    xt  = P.sbuf(S("xt"), [128, 1024], F32)
    yt  = P.sbuf(S("yt"), [128, 1024], F32)
    ot  = P.sbuf(S("ot"), [128, 1024], BF16)
    onesrf = P.sbuf(S("onesrf"), [1, 128], F32)
    pD  = P.psum(S("pD"), [128, 512], F32)
    pD2 = P.psum(S("pD2"), [128, 512], F32)
    pR  = P.psum(S("pR"), [128, 512], F32)
    ld = P.sem(S("ld")); ldh = P.sem(S("ldh")); ldw = P.sem(S("ldw")); ldw2 = P.sem(S("ldw2"))
    ldx2 = P.sem(S("ldx2")); ldy2 = P.sem(S("ldy2"))
    txw = P.sem(S("txw")); txo = P.sem(S("txo"))
    tsem = P.sem(S("tsem")); vsem = P.sem(S("vsem")); ssem = P.sem(S("ssem"))
    osem = P.sem(S("osem")); vf = P.sem(S("vf"))
    vfc = {"n": 0}
    def VF(v, instr):
        if P.paranoid: return
        vfc["n"] += 1; instr.then_inc(vf.h, 1); v.wait_ge(vf.h, vfc["n"])
    last = {}
    ldsc = P.sem(S("ldsc"))
    ldsc2 = P.sem(S("ldsc2"))
    tLc = E("sync", lambda sy: sy.dma_start(out=sc1[0:1, 0:1], in_=clipd[:, :]), inc=ldsc, inc_by=16)
    tL8 = E("sync", lambda sy: sy.dma_start(out=lm8[:, :], in_=lmaxh8[:, :]), inc=ldsc2, inc_by=16)
    E("vector", lambda v: v.memset(onesrf[:], 1.0), inc=vsem)
    tOnes = vsem.ticket()
    tSg = E("scalar", lambda s: s.activation(sc1[:, 3:4], sc1[:, 0:1], AF.Sigmoid), waits=(tLc,), inc=ssem)
    def shfn(v):
        VF(v, v.tensor_reduce(sc1[:, 0:1], lm8[:, :], axis=AX, op=ALU.max))
        VF(v, v.tensor_tensor(sc1[:, 1:2], sc1[:, 3:4], sc1[:, 0:1], op=ALU.mult))
        VF(v, v.tensor_scalar(sc1[:, 1:2], sc1[:, 1:2], 1.0 / QMAX, 0.0, op0=ALU.mult, op1=ALU.add))
        return v.reciprocal(sc1[:, 2:3], sc1[:, 1:2])
    tSh = E("vector", shfn, waits=(tL8, tSg), inc=vsem)
    tRep = E("tensor", lambda te: te.matmul(pR[:, 0:2], onesrf[:, :], sc1[:, 1:3], start=True, stop=True),
             waits=(tSh, tOnes), inc=tsem)
    tShc = E("vector", lambda v: v.tensor_copy(shc[:, :], pR[:, 0:2]), waits=(tRep,), inc=vsem)

    for dt_ in range(8):
        buf = dt_ % 2
        tWL = E("gpsimd", lambda g, dt_=dt_, buf=buf: g.dma_start(
            out=wt[:, 2816 * buf:2816 * (buf + 1)], in_=wd[128 * dt_:128 * (dt_ + 1), :]),
            waits=(last.get(f"wt{buf}_free"),), inc=(ldw if buf == 0 else ldw2), inc_by=16)
        def qfn(v, dt_=dt_, buf=buf):
            wtile = wt[:, 2816 * buf:2816 * (buf + 1)]
            VF(v, v.tensor_reduce(sdc[:, 8:9], wtile, axis=AX, op=ALU.max, apply_absolute_value=True))
            VF(v, v.tensor_scalar(sdc[:, dt_:dt_ + 1], sdc[:, 8:9], 1.0 / QMAX, 0.0, op0=ALU.mult, op1=ALU.add))
            VF(v, v.reciprocal(sdc[:, 9:10], sdc[:, dt_:dt_ + 1]))
            v.tensor_scalar(sq[:], wtile, sdc[:, 9:10], 0.0, op0=ALU.mult, op1=ALU.add)
            _round_chain(v, VF, cmp_[:], sq[:], scr[:], sq[:])
            return v.tensor_scalar(wdq[:], cmp_[:], QMAX, -QMAX - 1.0, op0=ALU.min, op1=ALU.max)
        tQ = E("vector", qfn, waits=(tWL, tShc if dt_ == 0 else None, last.get("wdq_free")), inc=vsem)
        last[f"wt{buf}_free"] = tQ
        for it in range(22):
            tTW = E("sync", lambda sy, dt_=dt_, it=it: sy.dma_start_transpose(
                wdqT[:, 1024 * it + 128 * dt_:1024 * it + 128 * dt_ + 128],
                wdq[:, 128 * it:128 * (it + 1)]),
                waits=(tQ,) if it == 0 else (), inc=txw, inc_by=16)
        last["wdq_free"] = tTW
    tWQ = tTW

    for tc in range(4):
        tHq = None
        for it in range(22):
            buf = it % 2
            tHL = E("gpsimd", lambda g, it=it, tc=tc, buf=buf: g.dma_start(
                out=wt[:, 2816 * buf:2816 * buf + 512],
                in_=hin[128 * it:128 * (it + 1), 512 * tc:512 * (tc + 1)]),
                waits=(last.get(f"wt{buf}_free"),), inc=(ldw if buf == 0 else ldw2), inc_by=16)
            def hqfn(v, it=it, buf=buf):
                htile = wt[:, 2816 * buf:2816 * buf + 512]
                v.tensor_scalar(sq[:, 0:512], htile, shc[:, 1:2], 0.0, op0=ALU.mult, op1=ALU.add)
                _round_chain(v, VF, cmp_[:, 0:512], sq[:, 0:512], scr[:, 0:512], scr[:, 512:1024])
                return v.tensor_scalar(hq[:, 512 * it:512 * (it + 1)], cmp_[:, 0:512],
                                       QMAX, -QMAX - 1.0, op0=ALU.min, op1=ALU.max)
            wq_ = [tHL, tWQ if (tc == 0 and it == 0) else None, last.get("hq_free") if it == 0 else None]
            tHq = E("vector", hqfn, waits=tuple(w for w in wq_ if w), inc=vsem)
            last[f"wt{buf}_free"] = tHq
        tEv = [None] * 8
        for dt_ in range(8):
            pb = pD if dt_ % 2 == 0 else pD2
            def dnfn(te, dt_=dt_, pb=pb):
                for it in range(22):
                    _i = te.matmul(pb[:, :], wdqT[:, 1024 * it + 128 * dt_:1024 * it + 128 * dt_ + 128],
                                   hq[:, 512 * it:512 * (it + 1)], start=(it == 0), stop=(it == 21))
                return _i
            wm = [tHq if dt_ == 0 else None, last.get(f"pD{dt_ % 2}_free")]
            tMM = E("tensor", dnfn, waits=tuple(w for w in wm if w), inc=tsem)
            def evfn(v, dt_=dt_, pb=pb):
                VF(v, v.tensor_tensor(sdc[:, 8:9], sdc[:, dt_:dt_ + 1], shc[:, 0:1], op=ALU.mult))
                return v.tensor_scalar(houT[:, 512 * dt_:512 * (dt_ + 1)], pb[:, :],
                                       sdc[:, 8:9], 0.0, op0=ALU.mult, op1=ALU.add)
            tEv[dt_] = E("vector", evfn, waits=(tMM, last.get("houT_free") if dt_ == 0 else None), inc=vsem)
            last[f"pD{dt_ % 2}_free"] = tEv[dt_]
        last["hq_free"] = tMM
        for ts_ in range(4):
            for ch in range(8):
                tTo = E("sync", lambda sy, ts_=ts_, ch=ch: sy.dma_start_transpose(
                    ot[:, 128 * ch:128 * (ch + 1)],
                    houT[:, 512 * ch + 128 * ts_:512 * ch + 128 * ts_ + 128]),
                    waits=(tEv[7], last.get("ot_free")) if ch == 0 else (), inc=txo, inc_by=16)
            ti = 4 * tc + ts_
            tLx = E("sync", lambda sy, ti=ti: sy.dma_start(out=xt[:], in_=x2r[128 * ti:128 * (ti + 1), :]),
                    waits=(last.get("xt_free"),), inc=ldx2, inc_by=16)
            tLy = E("sync", lambda sy, ti=ti: sy.dma_start(out=yt[:], in_=y3r[128 * ti:128 * (ti + 1), :]),
                    waits=(last.get("yt_free"),), inc=ldy2, inc_by=16)
            def outfn(v):
                v.tensor_tensor(xt[:], xt[:], yt[:], op=ALU.add)
                return v.tensor_tensor(xt[:], xt[:], ot[:], op=ALU.add)
            tOt = E("vector", outfn, waits=(tTo, tLx, tLy, last.get("busy2")), inc=vsem)
            tSo = E("scalar", lambda s, ti=ti: s.dma_start(out=dout[128 * ti:128 * (ti + 1), :], in_=xt[:]),
                    waits=(tOt,), inc=osem, inc_by=16)
            last["xt_free"] = tSo
            last["yt_free"] = tOt
            last["ot_free"] = tOt
            last["busy2"] = tOt
        last["houT_free"] = tTo
    P.barrier_wait("scalar", osem.ticket())
    return P


def build_c0(P):
    x2r = P.dram("x2r", [2048, 1024], F32, kind="ExternalInput")
    xlo = P.dram("xlo", [2048, 1024], F32, kind="ExternalOutput")
    S = lambda n: "c0_" + n
    E = P.emit
    xt = P.sbuf(S("xt"), [128, 1024], F32)
    sq = P.sbuf(S("sq"), [128, 1024], F32)
    mv = P.sbuf(S("mv"), [128, 4], F32)
    ld = P.sem(S("ld")); vsem = P.sem(S("vsem")); ssem = P.sem(S("ssem"))
    osem = P.sem(S("osem")); vf = P.sem(S("vf"))
    vfc = {"n": 0}
    def VF(v, instr):
        if P.paranoid: return
        vfc["n"] += 1; instr.then_inc(vf.h, 1); v.wait_ge(vf.h, vfc["n"])
    last = {}
    for t in range(16):
        tL = E("sync", lambda sy, t=t: sy.dma_start(out=xt[:], in_=x2r[128 * t:128 * (t + 1), :]),
               waits=(last.get("xt_free"),), inc=ld, inc_by=16)
        def f1(v, t=t):
            return _ln_tile(v, VF, AF, E, P, xt[:], sq[:], mv, 0)
        tF = E("vector", f1, waits=(tL, last.get("busy")), inc=vsem)
        tS = E("scalar", lambda s: s.activation(mv[:, 2:3], mv[:, 1:2], AF.Sqrt), waits=(tF,), inc=ssem)
        def f2(v, t=t):
            VF(v, v.reciprocal(mv[:, 2:3], mv[:, 2:3]))
            return v.tensor_scalar(xt[:], xt[:], mv[:, 2:3], 0.0, op0=ALU.mult, op1=ALU.add)
        tF2 = E("vector", f2, waits=(tS,), inc=vsem)
        tSo = E("scalar", lambda s, t=t: s.dma_start(out=xlo[128 * t:128 * (t + 1), :], in_=xt[:]),
                waits=(tF2,), inc=osem, inc_by=16)
        last["xt_free"] = tSo
        last["busy"] = tF2
    P.barrier_wait("scalar", osem.ticket())
    return P


import ml_dtypes
from concourse.bass_utils import run_bass_kernel_spmd

_NCACHE = {}


def _get(name, build):
    if name not in _NCACHE:
        P = Prog(name=name, n_cores=8)
        build(P)
        _NCACHE[name] = P.finalize()
    return _NCACHE[name]


def _run(name, build, in_maps):
    nc = _get(name, build)
    return run_bass_kernel_spmd(nc, in_maps, core_ids=list(range(8)))


def kernel(**kw):
    f32 = np.float32
    x = np.ascontiguousarray(np.asarray(kw["L0_attn_x"], f32))
    x_rows = x.reshape(16384, 1024)

    Wq = np.ascontiguousarray(np.asarray(kw["L0_attn_Wq"], f32))
    Wk = np.ascontiguousarray(np.asarray(kw["L0_attn_Wk"], f32))
    Wv = np.ascontiguousarray(np.asarray(kw["L0_attn_Wv"], f32))
    Wo = np.ascontiguousarray(np.asarray(kw["L0_attn_Wo"], f32))
    betas = np.asarray(kw["L0_attn_betas"], f32)
    iden = np.eye(128, dtype=np.float32).astype(ml_dtypes.bfloat16)
    idx_h = [(512 * np.arange(16)[:, None] + 32 * h + np.arange(32)[None, :]).ravel()
             for h in range(16)]
    in_maps = []
    for c in range(8):
        b = c // 4; h0 = 4 * (c % 4)
        xa = np.stack([x[b][idx_h[h0 + p]] for p in range(4)])
        bet = np.ascontiguousarray(betas[0, h0:h0 + 4, 0, :].T)
        in_maps.append({"xa": xa, "Wq": Wq, "Wk": Wk, "Wv": Wv, "Wo": Wo,
                        "betas_loc": bet, "iden": iden})
    resA = _run("A", build_l0_attn, in_maps)
    a_rows = np.empty((16384, 1024), f32)
    for c in range(8):
        b = c // 4; h0 = 4 * (c % 4)
        aout = np.asarray(resA.results[c]["aout"])
        for p in range(4):
            a_rows[8192 * b + idx_h[h0 + p]] = aout[p]

    perm = np.zeros(1536, np.int64)
    for h in range(16):
        kv = h // 4
        cq = (h % 4) * 2 + (h // 8)
        ko = 64 * (kv % 2)
        for j in range(32):
            perm[cq * 128 + ko + j] = h * 64 + 2 * j
            perm[cq * 128 + ko + 32 + j] = h * 64 + 2 * j + 1
    for kv in range(4):
        base = 1024 + (kv // 2) * 128 + 64 * (kv % 2)
        for j in range(32):
            perm[base + j] = 1024 + kv * 64 + 2 * j
            perm[base + 32 + j] = 1024 + kv * 64 + 2 * j + 1
    perm[1280:] = np.arange(1280, 1536)
    qkvw = np.ascontiguousarray(np.asarray(kw["L0_ffn_qkv_w"], f32)[:, perm])
    qkvb = np.ascontiguousarray(np.asarray(kw["L0_ffn_qkv_b"], f32)[perm])[None, :]
    projw = np.ascontiguousarray(np.asarray(kw["L0_ffn_proj_w"], f32))
    projb = np.asarray(kw["L0_ffn_proj_b"], f32)[None, :]
    anw = np.ascontiguousarray(np.asarray(kw["L0_ffn_attn_norm_w"], f32).reshape(8, 128).T)
    fnw = np.ascontiguousarray(np.asarray(kw["L0_ffn_ffn_norm_w"], f32).reshape(8, 128).T)
    gatew = np.ascontiguousarray(np.asarray(kw["L0_ffn_gate_w"], f32))
    gateb = np.asarray(kw["L0_ffn_gate_b"], f32)[None, :]
    Wup = np.ascontiguousarray(np.asarray(kw["L0_ffn_W_up"], f32))
    Wdn = np.ascontiguousarray(np.asarray(kw["L0_ffn_W_down"], f32))
    sink = np.asarray(kw["L0_ffn_attn_sink"], f32)[None, :]
    poss = np.arange(2176, dtype=np.float64) / 32.0
    inv = 1.0 / 150000.0 ** (np.arange(0, 64, 2, dtype=np.float64) / 64.0)
    ang = poss[:, None] * np.tile(inv, 32)[None, :]
    costab = np.cos(ang).astype(f32)
    sintab = np.sin(ang).astype(f32)
    NEGM = -30000.0
    qi = np.arange(128)[:, None]; kj = np.arange(128)[None, :]
    maskd = np.ascontiguousarray(np.where(kj > qi, NEGM, 0.0).astype(f32).T)
    maskl = np.ascontiguousarray(np.where(kj < qi, NEGM, 0.0).astype(f32).T)
    maskneg = np.full((128, 128), NEGM, f32)
    zrow = np.zeros((128, 1024), f32)
    in_maps = []
    for c in range(8):
        own = slice(2048 * c, 2048 * (c + 1))
        if c % 4 == 0:
            hx, ha, ml0 = zrow, zrow, maskneg
        else:
            hx = x_rows[2048 * c - 128:2048 * c]
            ha = a_rows[2048 * c - 128:2048 * c]
            ml0 = maskl
        xb = np.ascontiguousarray(np.concatenate([hx, x_rows[own]], 0))
        ab = np.ascontiguousarray(np.concatenate([ha, a_rows[own]], 0))
        in_maps.append({"xb": xb, "ab": ab, "qkvw": qkvw, "qkvb": qkvb,
                        "projw": projw, "projb": projb, "anw": anw, "fnw": fnw,
                        "gatew": gatew, "gateb": gateb, "Wup": Wup, "Wdn": Wdn,
                        "sink": sink, "costab": costab, "sintab": sintab,
                        "maskd": maskd, "maskl": maskl, "maskl0": ml0})
    resB = _run("B", build_l0_ffn, in_maps)
    x2_rows = np.concatenate([np.asarray(resB.results[c]["bout"]) for c in range(8)], 0)

    in_maps = [{"x2r": x2_rows[2048 * c:2048 * (c + 1)]} for c in range(8)]
    resC0 = _run("C0", build_c0, in_maps)
    xln2 = np.concatenate([np.asarray(resC0.results[c]["xlo"]) for c in range(8)], 0)

    rowsp = np.empty((16384, 1024), f32)
    rv = rowsp.reshape(128, 2, 64, 1024)
    rv[:, 0] = xln2[:8192].reshape(128, 64, 1024)
    rv[:, 1] = xln2[8192:].reshape(128, 64, 1024)
    XT = np.ascontiguousarray(rowsp.T)
    tl = np.concatenate([np.arange(64), np.arange(64)])
    maskL_ = np.ascontiguousarray((tl[:, None] > tl[None, :]).astype(f32))
    maskT_ = np.ascontiguousarray((tl[None, :] > tl[:, None]).astype(f32))
    W0 = np.asarray(kw["L1_attn_W0"], f32)
    Wkv = np.asarray(kw["L1_attn_Wkv"], f32)
    bkv = np.asarray(kw["L1_attn_bkv"], f32)
    eta = np.asarray(kw["L1_attn_eta"], f32).reshape(1, 1)
    in_maps = []
    for c in range(8):
        jb = 128 * c
        in_maps.append({"XT": XT, "X": rowsp,
                        "W0T": np.ascontiguousarray(W0[jb:jb + 128, :].T),
                        "WkvT": np.ascontiguousarray(Wkv[1024 + jb:1024 + jb + 128, :].T),
                        "bkvb": np.ascontiguousarray(bkv[1024 + jb:1024 + jb + 128][None, :]),
                        "eta": eta, "maskT": maskT_, "maskL": maskL_})
    resC = _run("C", build_l1_attn, in_maps)
    yperm = np.concatenate([np.asarray(resC.results[c]["yj"]) for c in range(8)], 1)
    y3_rows = np.empty((16384, 1024), f32)
    yv = yperm.reshape(128, 2, 64, 1024)
    y3_rows[:8192] = yv[:, 0].reshape(8192, 1024)
    y3_rows[8192:] = yv[:, 1].reshape(8192, 1024)

    clipx = np.asarray(kw["L1_ffn_clip_x"], f32).reshape(1, 1)
    clipd = np.asarray(kw["L1_ffn_clip_down"], f32).reshape(1, 1)
    wg = np.ascontiguousarray(np.asarray(kw["L1_ffn_w_gate"], f32))
    wu = np.ascontiguousarray(np.asarray(kw["L1_ffn_w_up"], f32))
    wd = np.ascontiguousarray(np.asarray(kw["L1_ffn_w_down"], f32))
    x2s = [x2_rows[2048 * c:2048 * (c + 1)] for c in range(8)]
    y3s = [np.ascontiguousarray(y3_rows[2048 * c:2048 * (c + 1)]) for c in range(8)]
    in_maps = [{"x2r": x2s[c], "y3r": y3s[c]} for c in range(8)]
    resD1 = _run("D1", build_d1, in_maps)
    lmax8 = np.concatenate([np.asarray(resD1.results[c]["lmax"]).reshape(1, 1)
                            for c in range(8)], 1)
    in_maps = [{"x2r": x2s[c], "y3r": y3s[c], "lmax8": lmax8, "clipx": clipx,
                "wg": wg, "wu": wu} for c in range(8)]
    resD2 = _run("D2", build_d2, in_maps)
    lmaxh8 = np.concatenate([np.asarray(resD2.results[c]["lmaxh"]).reshape(1, 1)
                             for c in range(8)], 1)
    in_maps = [{"x2r": x2s[c], "y3r": y3s[c], "hin": np.asarray(resD2.results[c]["hout"]),
                "lmaxh8": lmaxh8, "clipd": clipd, "wd": wd} for c in range(8)]
    resD3 = _run("D3", build_d3, in_maps)
    out = np.concatenate([np.asarray(resD3.results[c]["dout"]) for c in range(8)], 0)
    return out.reshape(2, 8192, 1024).astype(np.float32)
